# version 17
# baseline (speedup 1.0000x reference)
import numpy as np
import ml_dtypes
import bisect
import concourse.bass as bass
import concourse.mybir as mybir
from concourse.alu_op_type import AluOpType as OP
from concourse.bass_utils import run_bass_kernel_spmd
from contextlib import ExitStack

F32 = mybir.dt.float32
BF16 = mybir.dt.bfloat16
AF = mybir.ActivationFunctionType
AX = mybir.AxisListType

NCORE = 8
B, S, D = 32, 2048, 1024
H, DH, F = 16, 64, 256
HID = 4096
SEQ_PER_CORE = B // NCORE
TOK = SEQ_PER_CORE * S
NT = S // 512
NBLK = TOK // 128
RC = 12582912.0
EPS_LN = 1e-5
EPS_PF = 1e-8
EPS_NORM = 1e-8
EPS_Q = 1e-5
PHASE_LIMIT = 3
A_PASS2 = True
A_KVT = True
A_STAGE = 4
A_KP = True
DBG_OUTS = False

ENGINES = ["sync", "tensor", "vector", "scalar", "gpsimd"]


class Chan:

    def __init__(self, name, kind):
        self.name = name
        self.kind = kind
        self.h = None
        self.n = 0
        self.marked = set()


class Builder:
    def __init__(self):
        self.plan = {e: [] for e in ENGINES}
        self.ch = {}
        for e, c in [("sync", "sd"), ("gpsimd", "gd"), ("scalar", "ad")]:
            self.ch[c] = Chan(c, "dma")
        for e, c in [("tensor", "pe"), ("vector", "ve"), ("scalar", "ae"),
                     ("gpsimd", "gc")]:
            self.ch[c] = Chan(c, "eng")
        self.eng_chan = {"tensor": "pe", "vector": "ve", "scalar": "ae",
                         "gpsimd": "gc"}

    def op(self, engine, fn, waits=(), dma=None):
        wl = []
        for t in waits:
            if t is None:
                continue
            cname, v = t
            c = self.ch[cname]
            if c.kind == "eng":
                c.marked.add(v)
            wl.append((cname, v))
        if dma is not None:
            c = self.ch[dma]
            c.n += 16
            tok = (dma, c.n)
            self.plan[engine].append((fn, wl, dma, c.n, None))
            return tok
        cname = self.eng_chan.get(engine)
        if cname is None:
            self.plan[engine].append((fn, wl, None, None, None))
            return None
        c = self.ch[cname]
        c.n += 1
        self.plan[engine].append((fn, wl, None, None, c.n))
        return (cname, c.n)

    def snap(self, cname):
        c = self.ch[cname]
        if c.n == 0:
            return None
        return (cname, c.n)

    def emit(self, engine, eng):
        cname = self.eng_chan.get(engine)
        marked_sorted = None
        if cname is not None:
            marked_sorted = sorted(self.ch[cname].marked)
        for (fn, wl, dma, dmaval, seq) in self.plan[engine]:
            for (wc, wv) in wl:
                c = self.ch[wc]
                if c.kind == "eng":
                    ms = sorted(c.marked)
                    idx = bisect.bisect_left(ms, wv)
                    assert idx < len(ms) and ms[idx] == wv, f"unmarked wait {wc} {wv}"
                    eng.wait_ge(c.h, idx + 1)
                else:
                    eng.wait_ge(c.h, wv)
            instr = fn(eng)
            if dma is not None:
                instr.then_inc(self.ch[dma].h, 16)
            elif seq is not None and seq in self.ch[cname].marked:
                val = bisect.bisect_right(marked_sorted, seq)
                if instr is None:
                    eng.sem_inc(self.ch[cname].h, 1)
                else:
                    instr.then_inc(self.ch[cname].h, 1)


def build_nc():
    nc = bass.Bass("TRN2", target_bir_lowering=False, detect_race_conditions=False)

    x_h = nc.dram_tensor("x", [TOK, D], F32, kind="ExternalInput").ap()
    wq_h = nc.dram_tensor("wq", [D, D], F32, kind="ExternalInput").ap()
    wk_h = nc.dram_tensor("wk", [D, D], F32, kind="ExternalInput").ap()
    wv_h = nc.dram_tensor("wv", [D, D], F32, kind="ExternalInput").ap()
    wo_h = nc.dram_tensor("wo", [D, D], F32, kind="ExternalInput").ap()
    proj_h = nc.dram_tensor("proj", [DH, F], F32, kind="ExternalInput").ap()
    bo_h = nc.dram_tensor("bo_t", [128, D], F32, kind="ExternalInput").ap()
    n1w_h = nc.dram_tensor("n1w_t", [128, D], F32, kind="ExternalInput").ap()
    n2w_h = nc.dram_tensor("n2w_t", [128, HID], F32, kind="ExternalInput").ap()
    ab_h = nc.dram_tensor("ab_t", [128, 2], F32, kind="ExternalInput").ap()
    w1_h = nc.dram_tensor("w1", [HID, D], F32, kind="ExternalInput").ap()
    w2_h = nc.dram_tensor("w2", [D, HID], F32, kind="ExternalInput").ap()
    identb_h = nc.dram_tensor("identb", [128, 128], BF16, kind="ExternalInput").ap()
    identf_h = nc.dram_tensor("identf", [128, 128], F32, kind="ExternalInput").ap()
    sel_h = nc.dram_tensor("sel", [8, 8 * 128], BF16, kind="ExternalInput").ap()
    ones1_h = nc.dram_tensor("ones1", [128, 1], F32, kind="ExternalInput").ap()
    onesr_h = nc.dram_tensor("onesr", [1, 128], F32, kind="ExternalInput").ap()

    y_h = nc.dram_tensor("y", [TOK, D], F32, kind="ExternalOutput").ap()

    _ik = "ExternalOutput" if DBG_OUTS else "Internal"
    x2_h = nc.dram_tensor("x2_scr", [TOK, D], F32, kind=_ik).ap()
    w1qT_h = nc.dram_tensor("w1qT_scr", [D, HID], BF16, kind=_ik).ap()
    w2qT_h = nc.dram_tensor("w2qT_scr", [HID, D], BF16, kind=_ik).ap()
    norm_h = nc.dram_tensor("norm_scr", [2, 2, 8, 512], BF16, kind=_ik).ap()
    if DBG_OUTS:
        kvdbg_h = nc.dram_tensor("kv_dbg", [65, H * F], F32, kind="ExternalOutput").ap()
        lndbg_h = nc.dram_tensor("ln_dbg", [128, 8, 512], BF16, kind="ExternalOutput").ap()
        kpdbg_h = nc.dram_tensor("kp_dbg", [128, 4, 8 * F], BF16, kind="ExternalOutput").ap()
        vadbg_h = nc.dram_tensor("va_dbg", [128, 4, H * 65], BF16, kind="ExternalOutput").ap()
        kqdbg_h = nc.dram_tensor("kq_dbg", [128, 2, 512], BF16, kind="ExternalOutput").ap()
        lntdbg_h = nc.dram_tensor("lnt_dbg", [128, D], F32, kind="ExternalOutput").ap()
        xthdbg_h = nc.dram_tensor("xth_dbg", [128, 4, D], F32, kind="ExternalOutput").ap()
        stadbg_h = nc.dram_tensor("sta_dbg", [128, 24], F32, kind="ExternalOutput").ap()
        afdbg_h = nc.dram_tensor("af_dbg", [128, 8, 512], BF16, kind="ExternalOutput").ap()
        qpdbg_h = nc.dram_tensor("qp_dbg", [128, 2, 2, 512], BF16, kind="ExternalOutput").ap()

    x_t = x_h.rearrange("(t p) d -> t p d", p=128)
    y_t = y_h.rearrange("(t p) d -> t p d", p=128)
    x2_t = x2_h.rearrange("(t p) d -> t p d", p=128)
    w1_t = w1_h.rearrange("(t p) d -> t p d", p=128)
    w2_t = w2_h.rearrange("(t p) j -> t p j", p=128)
    w1qT_v = w1qT_h.rearrange("(c p) j -> p c j", p=128)
    w2qT_v = w2qT_h.rearrange("(c p) d -> p c d", p=128)
    wq_v = wq_h.rearrange("(c p) o -> p c o", p=128)
    wk_v = wk_h.rearrange("(c p) o -> p c o", p=128)
    wv_v = wv_h.rearrange("(c p) o -> p c o", p=128)
    wo_v = wo_h.rearrange("(c p) o -> p c o", p=128)

    bld = Builder()
    OPN = bld.op
    snap = bld.snap
    last_dmaT = [None]

    def dmaT(out_ap, in_ap, waits=()):
        w = [t for t in waits if t is not None]
        if last_dmaT[0] is not None:
            w.append(last_dmaT[0])
        tok = OPN("sync", lambda e: e.dma_start_transpose(out_ap, in_ap),
                  waits=w, dma="sd")
        last_dmaT[0] = tok
        return tok

    with ExitStack() as es:
        Sb = lambda *a: es.enter_context(nc.sbuf_tensor(*a))
        Ps = lambda *a: es.enter_context(nc.psum_tensor(*a))
        for c in bld.ch.values():
            c.h = es.enter_context(nc.semaphore(c.name))
        block = es.enter_context(nc.Block())

        identb = Sb("s_identb", [128, 128], BF16)
        identf = Sb("s_identf", [128, 128], F32)
        sel = Sb("s_sel", [8, 8 * 128], BF16)
        ones1 = Sb("s_ones1", [128, 1], F32)
        onesr = Sb("s_onesr", [1, 128], F32)
        ab = Sb("s_ab", [128, 2], F32)
        smallA = Sb("smallA", [128, 9], F32)
        SW1, MW1, SW2, MW2, SQA, SGA, T0, T1, RCC = range(9)
        stA = Sb("stA", [128, 24], F32)
        stB = Sb("stB", [128, 24], F32)
        actscr = Sb("actscr", [128, 2048], F32)

        def col(t, i):
            return t[:, i:i + 1]

        c_idb = OPN("sync", lambda e: e.dma_start(out=identb[:], in_=identb_h), dma="sd")
        c_idf = OPN("sync", lambda e: e.dma_start(out=identf[:], in_=identf_h), dma="sd")
        c_sel = OPN("sync", lambda e: e.dma_start(out=sel[:], in_=sel_h), dma="sd")
        c_o1 = OPN("sync", lambda e: e.dma_start(out=ones1[:], in_=ones1_h), dma="sd")
        c_or = OPN("sync", lambda e: e.dma_start(out=onesr[:], in_=onesr_h), dma="sd")
        c_ab = OPN("sync", lambda e: e.dma_start(out=ab[:], in_=ab_h), dma="sd")

        rc_col = col(smallA, RCC)
        v_rc = OPN("vector", lambda e: e.memset(rc_col, RC))
        t0 = col(smallA, T0); t1 = col(smallA, T1); t2 = col(smallA, T1)
        sqa = col(smallA, SQA); sga = col(smallA, SGA)
        v = OPN("vector", lambda e: e.tensor_scalar(t0, col(ab, 0), -1.0, None, OP.mult),
                waits=[c_ab])
        v = OPN("vector", lambda e: e.tensor_tensor(t0, t0, col(ab, 0), OP.max),
                waits=[v])
        v_sqa = OPN("scalar", lambda e: e.activation(sqa, t0, AF.Sqrt), waits=[v])
        v = OPN("vector", lambda e: e.tensor_scalar(t2, col(ab, 0), 0.0, None, OP.is_ge),
                waits=[v])
        v_sga = OPN("vector", lambda e: e.tensor_scalar(sga, t2, 2.0, -1.0, OP.mult, OP.add),
                    waits=[v])

        def rsqrt_col(dst, src, s1, s2, srctok=None):
            vv = OPN("vector", lambda e: e.reciprocal(s1, src),
                     waits=[srctok] if srctok else [])
            aa = OPN("scalar", lambda e: e.activation(dst, s1, AF.Sqrt), waits=[vv])
            return aa

        def ln_stats4(srcs, width, stT, i0, wait_lists, consumed=()):
            m4 = stT[:, i0:i0 + 4]; r4 = stT[:, i0 + 4:i0 + 8]
            mr4 = stT[:, i0 + 8:i0 + 12]; q4 = stT[:, i0 + 12:i0 + 16]
            s4 = stT[:, i0 + 16:i0 + 20]
            wl0 = [t for t in consumed if t is not None]
            vss = []
            aas = []
            for j in range(4):
                vss.append(OPN("vector", lambda e, j=j: e.tensor_reduce(
                    m4[:, j:j + 1], srcs[j], AX.X, OP.add),
                    waits=(wait_lists[j] + wl0) if j == 0 else wait_lists[j]))
                aas.append(OPN("scalar", lambda e, j=j: e.activation(
                    actscr[:, 0:width], srcs[j], AF.Square,
                    accum_out=q4[:, j:j + 1]),
                    waits=(wait_lists[j] + wl0) if j == 0 else wait_lists[j]))
            vm = OPN("vector", lambda e: e.tensor_scalar(m4, m4, 1.0 / width, None,
                                                         OP.mult), waits=[vss[-1]])
            vv = OPN("vector", lambda e: e.tensor_scalar(s4, q4, 1.0 / width, None,
                                                         OP.mult), waits=[aas[-1]])
            vq = OPN("vector", lambda e: e.tensor_tensor(q4, m4, m4, OP.mult), waits=[vm])
            vv = OPN("vector", lambda e: e.tensor_tensor(s4, s4, q4, OP.subtract),
                     waits=[vq, vv])
            vv = OPN("vector", lambda e: e.tensor_scalar(s4, s4, EPS_LN, None, OP.add),
                     waits=[vv])
            vv = OPN("vector", lambda e: e.reciprocal(q4, s4), waits=[vv])
            aa = OPN("scalar", lambda e: e.activation(r4, q4, AF.Sqrt), waits=[vv])
            vmr = OPN("vector", lambda e: e.tensor_scalar(mr4, m4, -1.0, None, OP.mult),
                      waits=[vm])
            tok = OPN("vector", lambda e: e.tensor_tensor(mr4, mr4, r4, OP.mult),
                      waits=[vmr, aa])
            outs = [(m4[:, j:j + 1], r4[:, j:j + 1], mr4[:, j:j + 1]) for j in range(4)]
            return tok, outs

        def ln_stats(src_ap, width, stT, i0, wait_list, eps=EPS_LN, consumed=()):
            m = col(stT, i0); r = col(stT, i0 + 1); mr = col(stT, i0 + 2)
            q1 = col(stT, i0 + 3); q2 = col(stT, i0 + 4)
            wl = list(wait_list) + [t for t in consumed if t is not None]
            vs = OPN("vector", lambda e: e.tensor_reduce(m, src_ap, AX.X, OP.add),
                     waits=wl)
            aa = OPN("scalar", lambda e: e.activation(actscr[:, 0:width],
                                                      src_ap, AF.Square, accum_out=q1),
                     waits=wl)
            vm = OPN("vector", lambda e: e.tensor_scalar(m, m, 1.0 / width, None,
                                                         OP.mult), waits=[vs])
            vv = OPN("vector", lambda e: e.tensor_scalar(q2, q1, 1.0 / width, None,
                                                         OP.mult), waits=[aa])
            vq = OPN("vector", lambda e: e.tensor_tensor(q1, m, m, OP.mult), waits=[vm])
            vv = OPN("vector", lambda e: e.tensor_tensor(q2, q2, q1, OP.subtract),
                     waits=[vq, vv])
            vv = OPN("vector", lambda e: e.tensor_scalar(q2, q2, eps, None, OP.add),
                     waits=[vv])
            vv = rsqrt_col(r, q2, q1, q2, srctok=vv)
            vmr = OPN("vector", lambda e: e.tensor_scalar(mr, m, -1.0, None, OP.mult),
                      waits=[vm])
            tok = OPN("vector", lambda e: e.tensor_tensor(mr, mr, r, OP.mult),
                      waits=[vmr, vv])
            return tok, m, r, mr

        def barrier():
            toks = [snap(c) for c in ["sd", "gd", "ad", "pe", "ve", "ae", "gc"]]
            toks = [t for t in toks if t is not None]
            for e in ENGINES:
                OPN(e, lambda eng: None, waits=toks)

        x2st = [None] * 64
        with ExitStack() as esp:
            SbP = lambda *a: esp.enter_context(nc.sbuf_tensor(*a))
            wld = SbP("wld", [128, 2, HID], F32)
            wqs = SbP("wqs", [128, HID], F32)
            wqb = SbP("wqb", [128, HID], BF16)
            wtp = SbP("wtp", [128, HID], BF16)
            acc32 = SbP("acc32", [128, 32], F32)
            accs = SbP("accs", [128, 1], F32)
            p_misc = esp.enter_context(nc.psum_tensor("p_misc", [128, 512], F32))

            def wquant(tiles, ntile, width, sw_slot, mw_slot, wqT_view, prev_busy=None,
                       prev_store=None):
                abs_toks = [None] * ntile
                ld_toks = [None] * ntile
                for t in range(ntile):
                    buf = wld[:, t % 2, 0:width]
                    w = [abs_toks[t - 2]] if t >= 2 else [prev_busy]
                    ld_toks[t] = OPN("gpsimd", lambda e, b=buf, t=t: e.dma_start(
                        out=b, in_=tiles[t]), waits=w, dma="gd")
                    abs_toks[t] = OPN("scalar", lambda e, b=buf, t=t, width=width:
                                      e.activation(wtp[:, 0:width], b, AF.Abs,
                                                   accum_out=col(acc32, t)),
                                      waits=[ld_toks[t]] +
                                            ([prev_store] if t == 0 else []))
                vr = OPN("vector", lambda e: e.tensor_reduce(
                    accs[:], acc32[:, 0:ntile], AX.X, OP.add), waits=[abs_toks[-1]])
                ms = OPN("tensor", lambda e: e.matmul(p_misc[0:1, 0:1], ones1[:],
                                                      accs[:], start=True, stop=True),
                         waits=[vr, c_o1])
                nelem = float(ntile * 128 * width)
                vm = OPN("vector", lambda e: e.tensor_scalar(
                    accs[0:1, :], p_misc[0:1, 0:1], 1.0 / nelem, EPS_Q, OP.mult, OP.max),
                    waits=[ms])
                mr2 = OPN("tensor", lambda e: e.matmul(p_misc[:, 1:2], onesr[:],
                                                       accs[0:1, :], start=True, stop=True),
                          waits=[vm, c_or])
                vmw = OPN("vector", lambda e: e.tensor_copy(
                    col(smallA, mw_slot), p_misc[:, 1:2]), waits=[mr2])
                vsw = OPN("vector", lambda e: e.reciprocal(
                    col(smallA, sw_slot), col(smallA, mw_slot)), waits=[vmw])
                st_tok = None
                tp_tok = None
                v3_prev = None
                for t in range(ntile):
                    buf = wld[:, t % 2, 0:width]
                    w = [v3_prev] if t >= 2 else [abs_toks[-2 + t]]
                    ld = OPN("gpsimd", lambda e, b=buf, t=t: e.dma_start(out=b, in_=tiles[t]),
                             waits=w, dma="gd")
                    v1 = OPN("vector", lambda e, b=buf: e.tensor_scalar(
                        wqs[:, 0:width], b, col(smallA, sw_slot), 1.0, OP.mult, OP.min),
                        waits=[ld, vsw] + ([tp_tok] if tp_tok else []))
                    v2 = OPN("vector", lambda e, width=width: e.tensor_scalar(
                        wqs[:, 0:width], wqs[:, 0:width], -1.0, RC, OP.max, OP.add),
                        waits=[v1])
                    v3 = OPN("vector", lambda e, width=width: e.tensor_scalar(
                        wqb[:, 0:width], wqs[:, 0:width], RC, None, OP.subtract),
                        waits=[v2])
                    v3_prev = v1
                    wquant._last_v1 = v1
                    tp_tok = dmaT(
                        wtp[:, 0:width].rearrange("p (c q) -> p c q", q=128),
                        wqb[:, 0:width], waits=[v3, st_tok])
                    st_tok = OPN("scalar", lambda e, t=t, width=width: e.dma_start(
                        out=wqT_view[:, :, t * 128:(t + 1) * 128],
                        in_=wtp[:, 0:width].rearrange("p (c q) -> p c q", q=128)),
                        waits=[tp_tok], dma="ad")
                return st_tok

            w1_done = wquant([w1_t[t] for t in range(32)], 32, D, SW1, MW1, w1qT_v)
            w2_done = wquant([w2_t[t] for t in range(8)], 8, HID, SW2, MW2, w2qT_v,
                             prev_busy=wquant._last_v1, prev_store=w1_done)
            barrier()

        def _phase_a():
          nonlocal x2st
          with ExitStack() as esa:
            SbA = lambda *a: esa.enter_context(nc.sbuf_tensor(*a))
            PsA = lambda *a: esa.enter_context(nc.psum_tensor(*a))
            WqS = SbA("WqS", [128, 8, D], BF16)
            WkS = SbA("WkS", [128, 8, D], BF16)
            WvS = SbA("WvS", [128, 8, D], BF16)
            WoS = SbA("WoS", [128, 8, D], BF16)
            proj2 = SbA("proj2", [128, F], BF16)
            bo_t = SbA("s_bo_t", [128, D], F32)
            ln1_fm = SbA("ln1_fm", [128, 8, 512], BF16)
            ln1_tm = SbA("ln1_tm", [128, 2, D], BF16)
            kqf = SbA("kqf", [128, 2, 512], BF16)
            kp_tm = SbA("kp_tm", [128, 4, 8 * F], BF16)
            v_aug = SbA("v_aug", [128, 4, H * 65], BF16)
            kv_acc = SbA("kv_acc", [65, H * F], F32)
            kv_bfT = SbA("kv_bfT", [128, 2 * H, 65], BF16)
            xth = SbA("xth", [128, 4, D], F32)
            xres = SbA("xres", [128, 2, D], F32)
            qpf = SbA("qpf", [128, 2, 2, 512], BF16)
            attn_fm = SbA("attn_fm", [128, 8, 512], BF16)
            nflat = SbA("nflat", [1, 16 * 512], BF16)
            n16 = SbA("n16", [8, 2, 512], BF16)
            n16f = SbA("n16f", [8, 2, 512], F32)
            g16 = SbA("g16", [8, 2, 512], BF16)
            x2o = SbA("x2o", [128, 2, D], F32)

            p_kv = PsA("p_kv", [128, 4 * 512], F32)
            p_a = PsA("p_a", [128, 1024], F32)
            p_b = PsA("p_b", [128, 1024], F32)

            def p_att(h):
                return p_kv[0:65, 512 * (h % 2):512 * (h % 2) + 512]

            ld_wq = OPN("gpsimd", lambda e: e.dma_start(out=WqS[:], in_=wq_v), dma="gd")
            ld_wk = OPN("gpsimd", lambda e: e.dma_start(out=WkS[:], in_=wk_v), dma="gd")
            ld_wv = OPN("gpsimd", lambda e: e.dma_start(out=WvS[:], in_=wv_v), dma="gd")
            ld_wo = OPN("gpsimd", lambda e: e.dma_start(out=WoS[:], in_=wo_v), dma="gd")
            ld_p1 = OPN("gpsimd", lambda e: e.dma_start(out=proj2[0:64, :], in_=proj_h), dma="gd")
            ld_p2 = OPN("gpsimd", lambda e: e.dma_start(out=proj2[64:128, :], in_=proj_h), dma="gd")
            ld_bo = OPN("sync", lambda e: e.dma_start(out=bo_t[:], in_=bo_h), dma="sd")
            ones_toks = []
            for sub in range(4):
                va = v_aug[:, sub].rearrange("p (h q) -> p h q", q=65)
                ones_toks.append(OPN("vector", lambda e, va=va: e.memset(va[:, :, 64:65], 1.0)))

            def emit_ln1(gt, prev_consumed, pa_free, lnfm_free=None):
                ldx = OPN("gpsimd", lambda e: e.dma_start(
                    out=xth[:], in_=x_t[4 * gt:4 * gt + 4].rearrange("j p d -> p j d")),
                    waits=list(prev_consumed), dma="gd")
                prev_ap = getattr(emit_ln1, "_prev_ap", None)
                tok, mrm = ln_stats4([xth[:, j] for j in range(4)], D, stA, 0,
                                     [[ldx]] * 4, consumed=[prev_ap])
                last_ap = None
                tpl_slot = getattr(emit_ln1, "_tpl", [None, None])
                ev = None
                for sub in range(4):
                    src = xth[:, sub]
                    m, r, mr = mrm[sub]
                    a_ap = OPN("scalar", lambda e, src=src, sub=sub, r=r, mr=mr:
                               e.activation(ln1_tm[:, sub % 2, :], src, AF.Identity,
                                            bias=mr, scale=r),
                               waits=[tok, tpl_slot[sub % 2]])
                    last_ap = a_ap
                    ev = dmaT(ln1_fm[:, :, 128 * sub:128 * sub + 128],
                              ln1_tm[:, sub % 2, :],
                              waits=[a_ap] + ([lnfm_free] if (sub == 0 and lnfm_free) else []))
                    tpl_slot[sub % 2] = ev
                emit_ln1._prev_ap = last_ap
                emit_ln1._tpl = tpl_slot
                return ev, ldx, (last_ap, tok)

            prev_ln_consumed = ()
            kvflush = None
            ev_kq_prev = None
            kv_mm_last = None
            pa_free = None
            lnfm_free = None
            vx2_slot = {}
            stx_slot = {}

            for sq in range(SEQ_PER_CORE):
                for t in range(NT):
                    gt = sq * NT + t
                    ev_ln, ldx, prev_ln_consumed = emit_ln1(gt, prev_ln_consumed, pa_free,
                                                            lnfm_free)
                    for wave in range(2 if A_STAGE >= 2 else 0):
                        evk_tok = None
                        for cc in range(4):
                            c = 4 * wave + cc
                            mk = None
                            for ic in range(8):
                                w = []
                                if ic == 0:
                                    w = [ev_ln, snap("gd")] if (t == 0 and wave == 0 and cc == 0 and sq == 0) else [ev_ln]
                                    if ev_kq_prev is not None:
                                        w.append(ev_kq_prev)
                                mk = OPN("tensor", lambda e, c=c, ic=ic, t=t: e.matmul(
                                    p_b[:, 512 * (c % 2):512 * (c % 2) + 512],
                                    WkS[:, ic, 128 * c:128 * (c + 1)],
                                    ln1_fm[:, ic, :],
                                    start=(ic == 0), stop=(ic == 7)),
                                    waits=(w if ic == 0 else []))
                            ev_kq = OPN("vector", lambda e, c=c: e.tensor_copy(
                                kqf[:, c % 2, :], p_b[:, 512 * (c % 2):512 * (c % 2) + 512]),
                                waits=[mk])
                            ev_kq_prev = ev_kq
                            if c == 7:
                                lnfm_free = mk
                            for sp in range(2 if A_KP else 0):
                                mkp = None
                                for su in range(2):
                                    sub = 2 * sp + su
                                    for par in range(2):
                                        w = []
                                        if su == 0 and par == 0:
                                            w = [ev_kq]
                                            if evk_tok is not None:
                                                w.append(evk_tok)
                                            if kv_mm_last is not None and cc == 0 and sp == 0:
                                                w.append(kv_mm_last)
                                        mkp = OPN("tensor", lambda e, c=c, sub=sub, par=par, su=su:
                                                  e.matmul(
                                            p_a[:, 512 * par + 256 * su:512 * par + 256 * su + 256],
                                            kqf[64 * par:64 * par + 64, c % 2,
                                                128 * sub:128 * sub + 128],
                                            proj2[64 * par:64 * par + 64, :],
                                            start=True, stop=True),
                                            waits=w)
                                hh = (2 * c) % 8
                                evk_tok = OPN("vector", lambda e, sp=sp, hh=hh: e.tensor_relu(
                                    kp_tm[:, 2 * sp:2 * sp + 2, 256 * hh:256 * hh + 512]
                                        .rearrange("p su (hd q) -> p su hd q", q=256),
                                    p_a[:].rearrange("p (par su q) -> p su par q",
                                                     par=2, su=2)),
                                    waits=[mkp])
                        if wave == 0 and A_STAGE >= 3:
                            evv = None
                            for sub in range(4):
                                mv = None
                                for ic in range(8):
                                    for hf in range(2):
                                        w = []
                                        if ic == 0 and hf == 0:
                                            w = [evk_tok if sub == 0 else evv, ev_ln]
                                        mv = OPN("tensor", lambda e, sub=sub, ic=ic, hf=hf, t=t:
                                                 e.matmul(
                                            p_a[:, 512 * hf:512 * hf + 512],
                                            ln1_fm[:, ic, 128 * sub:128 * sub + 128],
                                            WvS[:, ic, 512 * hf:512 * hf + 512],
                                            start=(ic == 0), stop=(ic == 7)),
                                            waits=(w if (ic == 0 and hf == 0) else []))
                                evv = OPN("vector", lambda e, sub=sub: e.tensor_copy(
                                    v_aug[:, sub].rearrange("p (h q) -> p h q", q=65)[:, :, 0:64],
                                    p_a[:].rearrange("p (h q) -> p h q", q=64)),
                                    waits=[mv, ones_toks[sub]])
                            evk_tok = evv
                        for hh in range(8 if A_STAGE >= 4 else 0):
                            h = 8 * wave + hh
                            off = 512 * (hh // 2) + 256 * (hh % 2)
                            for sub in range(4):
                                w = []
                                if hh == 0 and sub == 0:
                                    w = [evv if wave == 0 else evk_tok]
                                    if kvflush is not None:
                                        w.append(kvflush)
                                kv_mm_last = OPN("tensor", lambda e, h=h, hh=hh, off=off, sub=sub:
                                                 e.matmul(
                                    p_kv[0:65, off:off + 256],
                                    v_aug[:, sub, 65 * h:65 * h + 65],
                                    kp_tm[:, sub, 256 * hh:256 * hh + 256],
                                    start=(sub == 0), stop=(sub == 3)),
                                    waits=w)
                        for bank in range(4 if A_STAGE >= 4 else 0):
                            pair0 = 8 * wave + 2 * bank
                            dst = kv_acc[:, F * pair0:F * pair0 + 512]
                            src = p_kv[0:65, 512 * bank:512 * bank + 512]
                            w = [kv_mm_last] if bank == 0 else []
                            if t == 0:
                                kvflush = OPN("vector", lambda e, dst=dst, src=src:
                                              e.tensor_copy(dst, src), waits=w)
                            else:
                                kvflush = OPN("vector", lambda e, dst=dst, src=src:
                                              e.tensor_tensor(dst, dst, src, OP.add),
                                              waits=w)
                    if A_STAGE >= 3:
                        pa_free = evv
                    elif A_STAGE >= 2:
                        pa_free = evk_tok if A_KP else None
                    if DBG_OUTS and sq == 0 and t == NT - 1:
                        wd = [snap("ve"), snap("pe")]
                        OPN("sync", lambda e: e.dma_start(out=lndbg_h, in_=ln1_fm[:]), waits=wd, dma="sd")
                        OPN("sync", lambda e: e.dma_start(out=kpdbg_h, in_=kp_tm[:]), waits=wd, dma="sd")
                        OPN("sync", lambda e: e.dma_start(out=vadbg_h, in_=v_aug[:]), waits=wd, dma="sd")
                        OPN("sync", lambda e: e.dma_start(out=kqdbg_h, in_=kqf[:]), waits=wd, dma="sd")
                        OPN("sync", lambda e: e.dma_start(out=xthdbg_h, in_=xth[:]), waits=wd + [snap("gd")], dma="sd")
                        OPN("sync", lambda e: e.dma_start(out=stadbg_h, in_=stA[:]), waits=wd + [snap("ae")], dma="sd")

                if not A_KVT:
                    continue
                evt = None
                for grp in range(8):
                    trt = None
                    for j in range(4):
                        i = 4 * grp + j
                        w = ([kvflush, pa_free] if (grp == 0 and j == 0)
                             else ([evt] if (j == 0 and evt) else []))
                        trt = OPN("tensor", lambda e, i=i, j=j: e.transpose(
                            p_a[:, 128 * j:128 * j + 65],
                            kv_acc[:, 128 * i:128 * (i + 1)], identf[0:65, 0:65]),
                            waits=[t for t in w if t])
                    evt = OPN("vector", lambda e, grp=grp: e.tensor_copy(
                        kv_bfT[:, 4 * grp:4 * grp + 4, :],
                        p_a[:, 0:512].rearrange("p (j q) -> p j q", q=128)[:, :, 0:65]),
                        waits=[trt])
                pa_free = evt
                if DBG_OUTS and sq == 0:
                    OPN("sync", lambda e: e.dma_start(out=kvdbg_h, in_=kv_acc[:]),
                        waits=[kvflush], dma="sd")

                for t in range(NT if A_PASS2 else 0):
                    gt = sq * NT + t
                    ev_ln, ldx, prev_ln_consumed = emit_ln1(gt, prev_ln_consumed, pa_free,
                                                            lnfm_free)
                    ga_prev = getattr(emit_ln1, "_ga15", None)
                    ga_toks = [None] * 16
                    eva_toks = [None] * 16
                    evqp_prev = None
                    evqp_prev2 = {}
                    st_n_prev = getattr(emit_ln1, "_stn", None)
                    ev_q_prev = None
                    ma_tok = None
                    for c in range(8):
                        mq = None
                        for ic in range(8):
                            w = []
                            if ic == 0:
                                w = [ev_ln]
                                if ev_q_prev is not None:
                                    w.append(ev_q_prev)
                            mq = OPN("tensor", lambda e, c=c, ic=ic: e.matmul(
                                p_a[:, 512 * (c % 2):512 * (c % 2) + 512],
                                WqS[:, ic, 128 * c:128 * (c + 1)],
                                ln1_fm[:, ic, :],
                                start=(ic == 0), stop=(ic == 7)),
                                waits=(w if ic == 0 else []))
                        ev_q = OPN("vector", lambda e, c=c: e.tensor_copy(
                            kqf[:, c % 2, :], p_a[:, 512 * (c % 2):512 * (c % 2) + 512]),
                            waits=[mq])
                        ev_q_prev = ev_q
                        if c == 7:
                            lnfm_free = mq
                        for par in range(2):
                            h = 2 * c + par
                            mqp = None
                            qp_dst = p_b[:, 0:1024] if h % 2 == 0 else p_kv[:, 1024:2048]
                            for half in range(2):
                                w = []
                                if half == 0:
                                    w = [ev_q]
                                    if evqp_prev2.get(h % 2) is not None:
                                        w.append(evqp_prev2[h % 2])
                                mqp = OPN("tensor", lambda e, qp_dst=qp_dst, c=c, par=par, half=half:
                                          e.matmul(
                                    qp_dst[:, 512 * half:512 * half + 512],
                                    proj2[64 * par:64 * par + 64, 128 * half:128 * half + 128],
                                    kqf[64 * par:64 * par + 64, c % 2, :],
                                    start=True, stop=True), waits=w)
                            evqp = OPN("vector", lambda e, h=h, qp_dst=qp_dst: e.tensor_relu(
                                qpf[:, h % 2].rearrange("p a b -> p (a b)"), qp_dst),
                                waits=[mqp])
                            evqp_prev2[h % 2] = evqp
                            evqp_prev = evqp
                            for half in range(2):
                                w = []
                                if half == 0:
                                    w = [evqp]
                                    if h >= 2:
                                        w.append(eva_toks[h - 2])
                                        w.append(ga_toks[h - 2])
                                    elif ga_prev is not None:
                                        w.append(ga_prev)
                                ma_tok = OPN("tensor", lambda e, h=h, half=half: e.matmul(
                                    p_att(h), kv_bfT[:, 2 * h + half, :],
                                    qpf[:, h % 2, half, :],
                                    start=(half == 0), stop=(half == 1)), waits=w)
                            gw = [ma_tok]
                            if h == 0 and st_n_prev is not None:
                                gw.append(st_n_prev)
                            ga_toks[h] = OPN("scalar", lambda e, h=h: e.activation(
                                nflat[0:1, 512 * h:512 * h + 512], p_att(h)[64:65, :],
                                AF.Identity, bias=0.0, scale=1.0), waits=gw)
                            eva_toks[h] = OPN("vector", lambda e, h=h: e.tensor_copy(
                                attn_fm[64 * (h % 2):64 * (h % 2) + 64, h // 2, :],
                                p_att(h)[0:64, :]), waits=[ma_tok])
                    vg_half = [None, None]
                    prev_vg = getattr(emit_ln1, "_vg", None)
                    for hb in range(2):
                        st_n = OPN("sync", lambda e, gt=gt, hb=hb: e.dma_start(
                            out=norm_h[gt % 2, hb].rearrange("h t -> (h t)")[None, :],
                            in_=nflat[0:1, 4096 * hb:4096 * hb + 4096]),
                            waits=[ga_toks[8 * hb + 7]], dma="sd")
                        if hb == 1:
                            emit_ln1._stn = st_n
                            emit_ln1._ga15 = ga_toks[15]
                        ld_n = OPN("sync", lambda e, gt=gt, hb=hb: e.dma_start(
                            out=n16[:, hb, :], in_=norm_h[gt % 2, hb]),
                            waits=[st_n, prev_vg], dma="sd")
                        vg = OPN("vector", lambda e, hb=hb: e.tensor_scalar(
                            n16f[:, hb, :], n16[:, hb, :],
                            EPS_PF, None, OP.add), waits=[ld_n])
                        vg = OPN("vector", lambda e, hb=hb: e.reciprocal(
                            n16f[:, hb, :], n16f[:, hb, :]),
                            waits=[vg])
                        vg = OPN("vector", lambda e, hb=hb: e.tensor_copy(
                            g16[:, hb, :], n16f[:, hb, :]),
                            waits=[vg])
                        vg_half[hb] = vg
                    vg = vg_half[1]
                    emit_ln1._vg = vg
                    va_tok = None
                    va_list = []
                    for c in range(8):
                        w = ([vg_half[0], c_sel, evqp_prev2[0]] if c == 0 else
                             ([va_tok, vg_half[1]] if c == 4 else [va_tok])) + \
                            [eva_toks[2 * c], eva_toks[2 * c + 1]]
                        mg = OPN("tensor", lambda e, c=c: e.matmul(
                            p_b[:, 0:512], sel[:, 128 * c:128 * (c + 1)],
                            g16[:, c // 4, :],
                            start=True, stop=True), waits=w)
                        va_tok = OPN("vector", lambda e, c=c: e.tensor_tensor(
                            attn_fm[:, c, :], attn_fm[:, c, :], p_b[:, 0:512], OP.mult),
                            waits=[mg])
                        va_list.append(va_tok)
                    vz_prev = None
                    for sub in range(4):
                        ldxr = OPN("scalar", lambda e, gt=gt, sub=sub: e.dma_start(
                            out=xres[:, sub % 2], in_=x_t[4 * gt + sub]),
                            waits=[vx2_slot.get(sub % 2)], dma="ad")
                        mo = None
                        for ic in range(8):
                            for hf in range(2):
                                w = []
                                if hf == 0 and sub == 0:
                                    w = [va_list[ic]]
                                if ic == 0 and hf == 0 and vz_prev is not None:
                                    w.append(vz_prev)
                                mo = OPN("tensor", lambda e, sub=sub, ic=ic, hf=hf: e.matmul(
                                    p_a[:, 512 * hf:512 * hf + 512],
                                    attn_fm[:, ic, 128 * sub:128 * sub + 128],
                                    WoS[:, ic, 512 * hf:512 * hf + 512],
                                    start=(ic == 0), stop=(ic == 7)),
                                    waits=(w if w else []))

                        zb = actscr[:, 1024:2048]
                        vz = OPN("vector", lambda e: e.tensor_tensor(
                            zb, p_a[:], bo_t[:], OP.add), waits=[mo, ld_bo])
                        vz_prev = vz
                        tok2, m2, r2, mr2 = ln_stats(zb, D, stA, 6, [vz],
                                                     consumed=[getattr(emit_ln1, "_a2p", None)])
                        a2 = OPN("scalar", lambda e, r2=r2, mr2=mr2: e.activation(
                            zb, zb, AF.Identity, bias=mr2, scale=r2), waits=[tok2])
                        emit_ln1._a2p = a2
                        vx2 = OPN("vector", lambda e, sub=sub: e.tensor_tensor(
                            x2o[:, sub % 2], zb, xres[:, sub % 2], OP.add),
                            waits=[a2, ldxr, stx_slot.get(sub % 2)])
                        vx2_slot[sub % 2] = vx2
                        stx = OPN("sync", lambda e, gt=gt, sub=sub: e.dma_start(
                            out=x2_t[4 * gt + sub], in_=x2o[:, sub % 2]),
                            waits=[vx2], dma="sd")
                        stx_slot[sub % 2] = stx
                    x2st[gt] = stx
                    pa_free = vz_prev
                    if DBG_OUTS and gt == 0:
                        OPN("sync", lambda e: e.dma_start(out=afdbg_h, in_=attn_fm[:]),
                            waits=[va_tok], dma="sd")
                        OPN("sync", lambda e: e.dma_start(out=qpdbg_h, in_=qpf[:]),
                            waits=[va_tok], dma="sd")
            barrier()

        def _phase_b():
          with ExitStack() as esb:
            SbB = lambda *a: esb.enter_context(nc.sbuf_tensor(*a))
            PsB = lambda *a: esb.enter_context(nc.psum_tensor(*a))
            w1qT = SbB("w1qT", [128, 8, HID], BF16)
            w2qT = SbB("w2qT", [128, 32, D], BF16)
            n1w = SbB("n1w", [128, D], F32)
            n2w = SbB("n2w", [128, HID], F32)
            x2b = SbB("x2b", [128, 2, D], F32)
            shf = SbB("shf", [128, D], F32)
            y2f = SbB("y2f", [128, D], F32)
            xq_tm = SbB("xq_tm", [128, D], BF16)
            xq_fm = SbB("xq_fm", [128, 8, 128], BF16)
            g_sb = SbB("g_sb", [128, HID], BF16)
            u2b = SbB("u2b", [128, 2, 512], F32)
            hq_tm = SbB("hq_tm", [128, 2, 1024], BF16)
            hq_fm = SbB("hq_fm", [128, 32, 128], BF16)
            outb = SbB("outb", [128, 1, D], F32)

            p_h = PsB("p_h", [128, 4, 512], F32)
            p_y = PsB("p_y", [128, 1024], F32)
            p_x = PsB("p_x", [128, 1024], BF16)

            rrb2 = (actscr[:, 0:512], actscr[:, 512:1024])
            ld_w1q = OPN("sync", lambda e: e.dma_start(out=w1qT[:], in_=w1qT_v),
                         waits=[w1_done], dma="sd")
            ld_w2q = OPN("sync", lambda e: e.dma_start(out=w2qT[:], in_=w2qT_v),
                         waits=[w2_done], dma="sd")
            ld_n1 = OPN("sync", lambda e: e.dma_start(out=n1w[:], in_=n1w_h), dma="sd")
            ld_n2 = OPN("sync", lambda e: e.dma_start(out=n2w[:], in_=n2w_h), dma="sd")

            a_r_bank = [None] * 4
            a_y2_prev = [None]
            v_h3_last = [None]
            a_t_prev = {}
            tph_slot = [None, None]
            v_out_prev = {}
            my_prev = [None]
            st_y = None

            def part1(blk):
                bb = blk % 2
                ldb = OPN("gpsimd", lambda e, blk=blk, bb=bb: e.dma_start(
                    out=x2b[:, bb], in_=x2_t[blk]),
                    waits=[x2st[blk // 4], v_out_prev.get(bb), a_t_prev.get(bb)],
                    dma="gd")
                tok3, m3, r3, mr3 = ln_stats(x2b[:, bb], D, stB, 0, [ldb],
                                             consumed=[v_h3_last[0]])
                c1 = col(stB, 5); tq = col(stB, 6); tq2 = col(stB, 7)
                vv = OPN("vector", lambda e, r3=r3: e.tensor_tensor(tq, r3, r3, OP.mult),
                         waits=[tok3])
                vv = OPN("vector", lambda e: e.tensor_scalar(
                    tq, tq, -EPS_LN, 1.0 + EPS_NORM, OP.mult, OP.add), waits=[vv])
                v_c1 = rsqrt_col(c1, tq, tq2, col(stB, 8), srctok=vv)
                a_t = OPN("scalar", lambda e, r3=r3, mr3=mr3, bb=bb: e.activation(
                    shf[:], x2b[:, bb], AF.Identity, bias=mr3, scale=r3),
                    waits=[tok3, a_y2_prev[0]])
                a_t_prev[bb] = a_t
                v_u = OPN("vector", lambda e: e.tensor_tensor(shf[:], shf[:], n1w[:],
                                                              OP.mult),
                          waits=[a_t, ld_n1])
                am = col(stB, 9); A1 = col(stB, 10); cs = col(stB, 11); f1 = col(stB, 12)
                v_am = OPN("vector", lambda e: e.tensor_reduce(
                    am, shf[:], AX.X, OP.max, apply_absolute_value=True), waits=[v_u])
                vv = OPN("vector", lambda e: e.tensor_tensor(A1, c1, am, OP.mult),
                         waits=[v_c1, v_am])
                vv = OPN("vector", lambda e: e.tensor_scalar(A1, A1, EPS_Q, None, OP.max),
                         waits=[vv])
                vv = OPN("vector", lambda e: e.reciprocal(cs, A1), waits=[vv])
                vv = OPN("vector", lambda e: e.tensor_scalar(cs, cs, 127.0, None, OP.mult),
                         waits=[vv])
                v_cs = OPN("vector", lambda e: e.tensor_tensor(cs, cs, c1, OP.mult),
                           waits=[vv])
                vv = OPN("vector", lambda e: e.tensor_tensor(
                    f1, A1, col(smallA, MW1), OP.mult), waits=[v_cs])
                v_f1 = OPN("vector", lambda e: e.tensor_scalar(
                    f1, f1, 1.0 / 127.0, None, OP.mult), waits=[vv])
                a_xq = OPN("scalar", lambda e, cs=cs: e.activation(
                    actscr[:, 1024:2048], shf[:], AF.Identity, bias=rc_col, scale=cs),
                    waits=[v_cs, v_rc])
                v_xq = OPN("vector", lambda e: e.tensor_scalar(
                    xq_tm[:], actscr[:, 1024:2048], RC, None, OP.subtract), waits=[a_xq])
                trx = None
                for c in range(8):
                    w = [v_xq, c_idb] if c == 0 else []
                    trx = OPN("tensor", lambda e, c=c: e.transpose(
                        p_x[:, 128 * c:128 * (c + 1)], xq_tm[:, 128 * c:128 * (c + 1)],
                        identb[:]), waits=(w if c == 0 else []))
                v_xf = OPN("vector", lambda e: e.tensor_copy(
                    xq_fm[:].rearrange("p c q -> p (c q)"), p_x[:]), waits=[trx])
                return dict(bb=bb, f1=f1, v_f1=v_f1, v_xf=v_xf)

            def part23(blk, P):
                f1 = P["f1"]; v_f1 = P["v_f1"]; v_xf = P["v_xf"]
                sl_red = [None] * 8
                v_gs_sl = [None] * 8
                a_q_tok = None
                a_st_tok = None
                v_gs = None
                for s in range(8):
                    mh = None
                    for ic in range(8):
                        w = []
                        if ic == 0:
                            w = [v_xf, ld_w1q]
                            if a_r_bank[s % 4] is not None:
                                w.append(a_r_bank[s % 4])
                        mh = OPN("tensor", lambda e, s=s, ic=ic: e.matmul(
                            p_h[:, s % 4, :], xq_fm[:, ic, :],
                            w1qT[:, ic, 512 * s:512 * s + 512],
                            start=(ic == 0), stop=(ic == 7)),
                            waits=(w if ic == 0 else []))
                    sp = s % 2
                    a_r = OPN("scalar", lambda e, s=s, sp=sp, f1=f1: e.activation(
                        rrb2[sp], p_h[:, s % 4, :], AF.Relu, scale=f1),
                        waits=[mh, v_f1] + ([] if s >= 2 else
                                            [v_out_prev.get((blk - 1) % 2)]))
                    a_r_bank[s % 4] = a_r
                    a_q_tok = OPN("scalar", lambda e, sp=sp: e.activation(
                        u2b[:, sp, :], rrb2[sp], AF.Square, scale=sqa),
                        waits=[v_sqa] + ([v_gs_sl[s - 2]] if s >= 2 else []))
                    v_gs = OPN("vector", lambda e, s=s, sp=sp: e.tensor_scalar(
                        g_sb[:, 512 * s:512 * s + 512], u2b[:, sp, :], sga, col(ab, 1),
                        OP.mult, OP.add), waits=[a_q_tok, v_sga])
                    v_gs_sl[s] = v_gs
                    a_st_tok = OPN("scalar", lambda e, s=s: e.activation(
                        actscr[:, 0:512], g_sb[:, 512 * s:512 * s + 512],
                        AF.Square, accum_out=stB[:, 13 + s:14 + s]),
                        waits=[v_gs])
                    v_u2 = OPN("vector", lambda e, s=s, sp=sp: e.tensor_tensor(
                        actscr[:, 1024 + 512 * sp:1536 + 512 * sp],
                        g_sb[:, 512 * s:512 * s + 512],
                        n2w[:, 512 * s:512 * s + 512], OP.mult),
                        waits=[ld_n2, v_gs])
                    sl_red[s] = OPN("vector", lambda e, s=s, sp=sp: e.tensor_reduce(
                        stA[:, s:s + 1], actscr[:, 1024 + 512 * sp:1536 + 512 * sp],
                        AX.X, OP.max,
                        apply_absolute_value=True), waits=[v_u2])
                c2 = col(stB, 5); A2 = col(stB, 6); cs2 = col(stB, 7); f2 = col(stB, 18)
                tqa = col(stB, 21); tqb = col(stB, 22)
                vv = OPN("vector", lambda e: e.tensor_reduce(
                    tqa, stB[:, 13:21], AX.X, OP.add), waits=[a_st_tok, sl_red[7]])
                vv = OPN("vector", lambda e: e.tensor_scalar(
                    tqa, tqa, 1.0 / HID, EPS_NORM, OP.mult, OP.add), waits=[vv])
                v_c2 = rsqrt_col(c2, tqa, tqb, col(stB, 23), srctok=vv)
                vA = OPN("vector", lambda e: e.tensor_reduce(
                    A2, stA[:, 0:8], AX.X, OP.max), waits=[sl_red[7]])
                vv = OPN("vector", lambda e: e.tensor_tensor(A2, A2, c2, OP.mult),
                         waits=[v_c2, vA])
                vv = OPN("vector", lambda e: e.tensor_scalar(A2, A2, EPS_Q, None, OP.max),
                         waits=[vv])
                vv = OPN("vector", lambda e: e.reciprocal(cs2, A2), waits=[vv])
                vv = OPN("vector", lambda e: e.tensor_scalar(cs2, cs2, 127.0, None,
                                                             OP.mult), waits=[vv])
                v_cs2 = OPN("vector", lambda e: e.tensor_tensor(cs2, cs2, c2, OP.mult),
                            waits=[vv])
                vv = OPN("vector", lambda e: e.tensor_tensor(
                    f2, A2, col(smallA, MW2), OP.mult), waits=[v_cs2])
                v_f2 = OPN("vector", lambda e: e.tensor_scalar(
                    f2, f2, 1.0 / 127.0, None, OP.mult), waits=[vv])
                tph_pair = [None] * 4
                for s in range(8):
                    sp = s % 2
                    w = [v_cs2] if s == 0 else []
                    v_h1 = OPN("vector", lambda e, s=s, sp=sp, cs2=cs2:
                               e.scalar_tensor_tensor(
                        u2b[:, sp, :], g_sb[:, 512 * s:512 * s + 512], cs2,
                        n2w[:, 512 * s:512 * s + 512], OP.mult, OP.mult), waits=w)
                    p2 = s // 2
                    v_h3 = OPN("vector", lambda e, s=s, sp=sp, p2=p2: e.tensor_scalar(
                        hq_tm[:, p2 % 2, 512 * (s % 2):512 * (s % 2) + 512],
                        u2b[:, sp, :], RC, RC, OP.add, OP.subtract),
                        waits=[v_h1, tph_slot[p2 % 2]])
                    v_h3_last[0] = v_h3
                    if s % 2 == 1:
                        wt = [v_h3]
                        if p2 <= 1 and my_prev[0] is not None:
                            wt.append(my_prev[0])
                        tph = dmaT(hq_fm[:, 8 * p2:8 * p2 + 8, :],
                                   hq_tm[:, p2 % 2, :], waits=wt)
                        tph_slot[p2 % 2] = tph
                        tph_pair[p2] = tph
                return dict(tph_pair=tph_pair, v_f2=v_f2, f2=f2)

            def part4a(blk, P2):
                my = None
                for ic in range(32):
                    for hf in range(2):
                        w = []
                        if hf == 0 and ic % 8 == 0:
                            w = [P2["tph_pair"][ic // 8]]
                            if ic == 0:
                                w.append(ld_w2q)
                                if a_y2_prev[0] is not None:
                                    w.append(a_y2_prev[0])
                        my = OPN("tensor", lambda e, ic=ic, hf=hf: e.matmul(
                            p_y[:, 512 * hf:512 * hf + 512], hq_fm[:, ic, :],
                            w2qT[:, ic, 512 * hf:512 * hf + 512],
                            start=(ic == 0), stop=(ic == 31)),
                            waits=(w if w else []))
                my_prev[0] = my
                return my

            def part4b(blk, P, P2, my):
                nonlocal st_y
                bb = P["bb"]
                a_y2 = OPN("scalar", lambda e, f2=P2["f2"]: e.activation(
                    y2f[:], p_y[:], AF.Identity, scale=f2), waits=[my, P2["v_f2"]])
                a_y2_prev[0] = a_y2
                tok4, m4, r4, mr4 = ln_stats(y2f[:], D, stB, 13, [a_y2])
                a_y4 = OPN("scalar", lambda e, r4=r4, mr4=mr4: e.activation(
                    actscr[:, 0:1024], y2f[:], AF.Identity, bias=mr4, scale=r4),
                    waits=[tok4])
                v_out = OPN("vector", lambda e, bb=bb: e.tensor_tensor(
                    outb[:, 0], actscr[:, 0:1024], x2b[:, bb], OP.add),
                    waits=[a_y4, st_y])
                v_out_prev[bb] = v_out
                st_y = OPN("scalar", lambda e, blk=blk: e.dma_start(
                    out=y_t[blk], in_=outb[:, 0]), waits=[v_out], dma="ad")

            P = part1(0)
            for blk in range(NBLK):
                P2 = part23(blk, P)
                my = part4a(blk, P2)
                Pn = part1(blk + 1) if blk + 1 < NBLK else None
                part4b(blk, P, P2, my)
                P = Pn

        if PHASE_LIMIT >= 2:
            _phase_a()
        if PHASE_LIMIT >= 3:
            _phase_b()
        OPN("sync", lambda e: None,
            waits=[snap("sd"), snap("ad"), snap("gd")])
        OPN("scalar", lambda e: None, waits=[snap("ad")])

        @block.sync
        def _(sync):
            bld.emit("sync", sync)

        @block.tensor
        def _(tensor):
            bld.emit("tensor", tensor)

        @block.vector
        def _(vector):
            bld.emit("vector", vector)

        @block.scalar
        def _(scalar):
            bld.emit("scalar", scalar)

        @block.gpsimd
        def _(gpsimd):
            bld.emit("gpsimd", gpsimd)

    return nc


def kernel(**inputs):
    x = np.ascontiguousarray(np.asarray(inputs["L0_attn_hidden_states"],
                                        dtype=np.float32).reshape(B * S, D))
    shards = np.split(x, NCORE, axis=0)
    bo_t = np.tile(np.asarray(inputs["L0_attn_bo"], np.float32).reshape(1, D), (128, 1))
    n1w_t = np.tile(np.asarray(inputs["L0_ffn_norm1_w"], np.float32).reshape(1, D), (128, 1))
    n2w_t = np.tile(np.asarray(inputs["L0_ffn_norm2_w"], np.float32).reshape(1, HID), (128, 1))
    a_s = np.asarray(inputs["L0_ffn_act_scale"], np.float32).reshape(1)
    b_s = np.asarray(inputs["L0_ffn_act_bias"], np.float32).reshape(1)
    ab_t = np.tile(np.concatenate([a_s, b_s]).reshape(1, 2), (128, 1))
    identb = np.eye(128, dtype=ml_dtypes.bfloat16)
    identf = np.eye(128, dtype=np.float32)
    sel = np.zeros((8, 8 * 128), dtype=ml_dtypes.bfloat16)
    for c in range(8):
        for p in range(128):
            sel[(2 * c + (p >= 64)) % 8, 128 * c + p] = 1.0
    common = {
        "wq": np.ascontiguousarray(inputs["L0_attn_Wq"], dtype=np.float32),
        "wk": np.ascontiguousarray(inputs["L0_attn_Wk"], dtype=np.float32),
        "wv": np.ascontiguousarray(inputs["L0_attn_Wv"], dtype=np.float32),
        "wo": np.ascontiguousarray(inputs["L0_attn_Wo"], dtype=np.float32),
        "proj": np.ascontiguousarray(inputs["L0_attn_proj"], dtype=np.float32),
        "bo_t": np.ascontiguousarray(bo_t), "n1w_t": np.ascontiguousarray(n1w_t),
        "n2w_t": np.ascontiguousarray(n2w_t), "ab_t": np.ascontiguousarray(ab_t),
        "w1": np.ascontiguousarray(inputs["L0_ffn_w1"], dtype=np.float32),
        "w2": np.ascontiguousarray(inputs["L0_ffn_w2"], dtype=np.float32),
        "identb": identb, "identf": identf, "sel": sel,
        "ones1": np.ones((128, 1), np.float32), "onesr": np.ones((1, 128), np.float32),
    }
    nc = build_nc()
    in_maps = [dict(common, x=np.ascontiguousarray(s)) for s in shards]
    res = run_bass_kernel_spmd(nc, in_maps, core_ids=list(range(NCORE)))
    out = np.concatenate([np.asarray(r["y"]) for r in res.results], axis=0)
    return out.reshape(B, S, D)


# version 21
# speedup vs baseline: 1.2455x; 1.2455x over previous
import numpy as np
import ml_dtypes
import bisect
import concourse.bass as bass
import concourse.mybir as mybir
from concourse.alu_op_type import AluOpType as OP
from concourse.bass_utils import run_bass_kernel_spmd
from contextlib import ExitStack

F32 = mybir.dt.float32
BF16 = mybir.dt.bfloat16
AF = mybir.ActivationFunctionType
AX = mybir.AxisListType

NCORE = 8
B, S, D = 32, 2048, 1024
H, DH, F = 16, 64, 256
HID = 4096
SEQ_PER_CORE = B // NCORE
TOK = SEQ_PER_CORE * S
NT = S // 512
NBLK = TOK // 128
RC = 12582912.0
EPS_LN = 1e-5
EPS_PF = 1e-8
EPS_NORM = 1e-8
EPS_Q = 1e-5
PHASE_LIMIT = 3
A_PASS2 = True
A_KVT = True
A_STAGE = 4
A_KP = True
DBG_OUTS = False

ENGINES = ["sync", "tensor", "vector", "scalar", "gpsimd"]


class Chan:

    def __init__(self, name, kind):
        self.name = name
        self.kind = kind
        self.h = None
        self.n = 0
        self.marked = set()


class Builder:
    def __init__(self):
        self.plan = {e: [] for e in ENGINES}
        self.ch = {}
        for e, c in [("sync", "sd"), ("gpsimd", "gd"), ("scalar", "ad")]:
            self.ch[c] = Chan(c, "dma")
        for e, c in [("tensor", "pe"), ("vector", "ve"), ("scalar", "ae"),
                     ("gpsimd", "gc")]:
            self.ch[c] = Chan(c, "eng")
        self.eng_chan = {"tensor": "pe", "vector": "ve", "scalar": "ae",
                         "gpsimd": "gc"}

    def op(self, engine, fn, waits=(), dma=None):
        wl = []
        for t in waits:
            if t is None:
                continue
            cname, v = t
            c = self.ch[cname]
            if c.kind == "eng":
                c.marked.add(v)
            wl.append((cname, v))
        if dma is not None:
            c = self.ch[dma]
            c.n += 16
            tok = (dma, c.n)
            self.plan[engine].append((fn, wl, dma, c.n, None))
            return tok
        cname = self.eng_chan.get(engine)
        if cname is None:
            self.plan[engine].append((fn, wl, None, None, None))
            return None
        c = self.ch[cname]
        c.n += 1
        self.plan[engine].append((fn, wl, None, None, c.n))
        return (cname, c.n)

    def snap(self, cname):
        c = self.ch[cname]
        if c.n == 0:
            return None
        return (cname, c.n)

    def emit(self, engine, eng):
        cname = self.eng_chan.get(engine)
        marked_sorted = None
        if cname is not None:
            marked_sorted = sorted(self.ch[cname].marked)
        for (fn, wl, dma, dmaval, seq) in self.plan[engine]:
            for (wc, wv) in wl:
                c = self.ch[wc]
                if c.kind == "eng":
                    ms = sorted(c.marked)
                    idx = bisect.bisect_left(ms, wv)
                    assert idx < len(ms) and ms[idx] == wv, f"unmarked wait {wc} {wv}"
                    eng.wait_ge(c.h, idx + 1)
                else:
                    eng.wait_ge(c.h, wv)
            instr = fn(eng)
            if dma is not None:
                instr.then_inc(self.ch[dma].h, 16)
            elif seq is not None and seq in self.ch[cname].marked:
                val = bisect.bisect_right(marked_sorted, seq)
                if instr is None:
                    eng.sem_inc(self.ch[cname].h, 1)
                else:
                    instr.then_inc(self.ch[cname].h, 1)


def build_nc():
    nc = bass.Bass("TRN2", target_bir_lowering=False, detect_race_conditions=False)

    x_h = nc.dram_tensor("x", [TOK, D], F32, kind="ExternalInput").ap()
    wq_h = nc.dram_tensor("wq", [D, D], F32, kind="ExternalInput").ap()
    wk_h = nc.dram_tensor("wk", [D, D], F32, kind="ExternalInput").ap()
    wv_h = nc.dram_tensor("wv", [D, D], F32, kind="ExternalInput").ap()
    wo_h = nc.dram_tensor("wo", [D, D], F32, kind="ExternalInput").ap()
    proj_h = nc.dram_tensor("proj", [DH, F], F32, kind="ExternalInput").ap()
    bo_h = nc.dram_tensor("bo_t", [128, D], F32, kind="ExternalInput").ap()
    n1w_h = nc.dram_tensor("n1w_t", [128, D], F32, kind="ExternalInput").ap()
    n2w_h = nc.dram_tensor("n2w_t", [128, HID], F32, kind="ExternalInput").ap()
    ab_h = nc.dram_tensor("ab_t", [128, 2], F32, kind="ExternalInput").ap()
    w1_h = nc.dram_tensor("w1", [HID, D], F32, kind="ExternalInput").ap()
    w2_h = nc.dram_tensor("w2", [D, HID], F32, kind="ExternalInput").ap()
    identb_h = nc.dram_tensor("identb", [128, 128], BF16, kind="ExternalInput").ap()
    identf_h = nc.dram_tensor("identf", [128, 128], F32, kind="ExternalInput").ap()
    sel_h = nc.dram_tensor("sel", [8, 8 * 128], BF16, kind="ExternalInput").ap()
    ones1_h = nc.dram_tensor("ones1", [128, 1], F32, kind="ExternalInput").ap()
    onesr_h = nc.dram_tensor("onesr", [1, 128], F32, kind="ExternalInput").ap()

    y_h = nc.dram_tensor("y", [TOK, D], F32, kind="ExternalOutput").ap()

    _ik = "ExternalOutput" if DBG_OUTS else "Internal"
    x2_h = nc.dram_tensor("x2_scr", [TOK, D], F32, kind=_ik).ap()
    w1qT_h = nc.dram_tensor("w1qT_scr", [D, HID], BF16, kind=_ik).ap()
    w2qT_h = nc.dram_tensor("w2qT_scr", [HID, D], BF16, kind=_ik).ap()
    norm_h = nc.dram_tensor("norm_scr", [2, 2, 8, 512], BF16, kind=_ik).ap()
    if DBG_OUTS:
        kvdbg_h = nc.dram_tensor("kv_dbg", [65, H * F], F32, kind="ExternalOutput").ap()
        lndbg_h = nc.dram_tensor("ln_dbg", [128, 8, 512], BF16, kind="ExternalOutput").ap()
        kpdbg_h = nc.dram_tensor("kp_dbg", [128, 4, 8 * F], BF16, kind="ExternalOutput").ap()
        vadbg_h = nc.dram_tensor("va_dbg", [128, 4, H * 65], BF16, kind="ExternalOutput").ap()
        kqdbg_h = nc.dram_tensor("kq_dbg", [128, 2, 512], BF16, kind="ExternalOutput").ap()
        lntdbg_h = nc.dram_tensor("lnt_dbg", [128, D], F32, kind="ExternalOutput").ap()
        xthdbg_h = nc.dram_tensor("xth_dbg", [128, 4, D], F32, kind="ExternalOutput").ap()
        stadbg_h = nc.dram_tensor("sta_dbg", [128, 24], F32, kind="ExternalOutput").ap()
        afdbg_h = nc.dram_tensor("af_dbg", [128, 8, 512], BF16, kind="ExternalOutput").ap()
        qpdbg_h = nc.dram_tensor("qp_dbg", [128, 2, 2, 512], BF16, kind="ExternalOutput").ap()

    x_t = x_h.rearrange("(t p) d -> t p d", p=128)
    y_t = y_h.rearrange("(t p) d -> t p d", p=128)
    x2_t = x2_h.rearrange("(t p) d -> t p d", p=128)
    w1_t = w1_h.rearrange("(t p) d -> t p d", p=128)
    w2_t = w2_h.rearrange("(t p) j -> t p j", p=128)
    w1qT_v = w1qT_h.rearrange("(c p) j -> p c j", p=128)
    w2qT_v = w2qT_h.rearrange("(c p) d -> p c d", p=128)
    wq_v = wq_h.rearrange("(c p) o -> p c o", p=128)
    wk_v = wk_h.rearrange("(c p) o -> p c o", p=128)
    wv_v = wv_h.rearrange("(c p) o -> p c o", p=128)
    wo_v = wo_h.rearrange("(c p) o -> p c o", p=128)

    bld = Builder()
    OPN = bld.op
    snap = bld.snap
    last_dmaT = [None]

    def dmaT(out_ap, in_ap, waits=()):
        w = [t for t in waits if t is not None]
        if last_dmaT[0] is not None:
            w.append(last_dmaT[0])
        tok = OPN("sync", lambda e: e.dma_start_transpose(out_ap, in_ap),
                  waits=w, dma="sd")
        last_dmaT[0] = tok
        return tok

    with ExitStack() as es:
        Sb = lambda *a: es.enter_context(nc.sbuf_tensor(*a))
        Ps = lambda *a: es.enter_context(nc.psum_tensor(*a))
        for c in bld.ch.values():
            c.h = es.enter_context(nc.semaphore(c.name))
        block = es.enter_context(nc.Block())

        identb = Sb("s_identb", [128, 128], BF16)
        identf = Sb("s_identf", [128, 128], F32)
        sel = Sb("s_sel", [8, 8 * 128], BF16)
        ones1 = Sb("s_ones1", [128, 1], F32)
        onesr = Sb("s_onesr", [1, 128], F32)
        ab = Sb("s_ab", [128, 2], F32)
        smallA = Sb("smallA", [128, 9], F32)
        SW1, MW1, SW2, MW2, SQA, SGA, T0, T1, RCC = range(9)
        stA = Sb("stA", [128, 24], F32)
        stB = Sb("stB", [128, 24], F32)
        actscr = Sb("actscr", [128, 2048], F32)

        def col(t, i):
            return t[:, i:i + 1]

        c_idb = OPN("sync", lambda e: e.dma_start(out=identb[:], in_=identb_h), dma="sd")
        c_idf = OPN("sync", lambda e: e.dma_start(out=identf[:], in_=identf_h), dma="sd")
        c_sel = OPN("sync", lambda e: e.dma_start(out=sel[:], in_=sel_h), dma="sd")
        c_o1 = OPN("sync", lambda e: e.dma_start(out=ones1[:], in_=ones1_h), dma="sd")
        c_or = OPN("sync", lambda e: e.dma_start(out=onesr[:], in_=onesr_h), dma="sd")
        c_ab = OPN("sync", lambda e: e.dma_start(out=ab[:], in_=ab_h), dma="sd")

        rc_col = col(smallA, RCC)
        v_rc = OPN("vector", lambda e: e.memset(rc_col, RC))
        t0 = col(smallA, T0); t1 = col(smallA, T1); t2 = col(smallA, T1)
        sqa = col(smallA, SQA); sga = col(smallA, SGA)
        v = OPN("vector", lambda e: e.tensor_scalar(t0, col(ab, 0), -1.0, None, OP.mult),
                waits=[c_ab])
        v = OPN("vector", lambda e: e.tensor_tensor(t0, t0, col(ab, 0), OP.max),
                waits=[v])
        v_sqa = OPN("scalar", lambda e: e.activation(sqa, t0, AF.Sqrt), waits=[v])
        v = OPN("vector", lambda e: e.tensor_scalar(t2, col(ab, 0), 0.0, None, OP.is_ge),
                waits=[v])
        v_sga = OPN("vector", lambda e: e.tensor_scalar(sga, t2, 2.0, -1.0, OP.mult, OP.add),
                    waits=[v])

        def rsqrt_col(dst, src, s1, s2, srctok=None):
            vv = OPN("vector", lambda e: e.reciprocal(s1, src),
                     waits=[srctok] if srctok else [])
            aa = OPN("scalar", lambda e: e.activation(dst, s1, AF.Sqrt), waits=[vv])
            return aa

        def ln_stats4(srcs, width, stT, i0, wait_lists, consumed=()):
            m4 = stT[:, i0:i0 + 4]; r4 = stT[:, i0 + 4:i0 + 8]
            mr4 = stT[:, i0 + 8:i0 + 12]; q4 = stT[:, i0 + 12:i0 + 16]
            s4 = stT[:, i0 + 16:i0 + 20]
            wl0 = [t for t in consumed if t is not None]
            vss = []
            aas = []
            for j in range(4):
                vss.append(OPN("vector", lambda e, j=j: e.tensor_reduce(
                    m4[:, j:j + 1], srcs[j], AX.X, OP.add),
                    waits=(wait_lists[j] + wl0) if j == 0 else wait_lists[j]))
                aas.append(OPN("scalar", lambda e, j=j: e.activation(
                    actscr[:, 0:width], srcs[j], AF.Square,
                    accum_out=q4[:, j:j + 1]),
                    waits=(wait_lists[j] + wl0) if j == 0 else wait_lists[j]))
            vm = OPN("vector", lambda e: e.tensor_scalar(m4, m4, 1.0 / width, None,
                                                         OP.mult), waits=[vss[-1]])
            vv = OPN("vector", lambda e: e.tensor_scalar(s4, q4, 1.0 / width, None,
                                                         OP.mult), waits=[aas[-1]])
            vq = OPN("vector", lambda e: e.tensor_tensor(q4, m4, m4, OP.mult), waits=[vm])
            vv = OPN("vector", lambda e: e.tensor_tensor(s4, s4, q4, OP.subtract),
                     waits=[vq, vv])
            vv = OPN("vector", lambda e: e.tensor_scalar(s4, s4, EPS_LN, None, OP.add),
                     waits=[vv])
            vv = OPN("vector", lambda e: e.reciprocal(q4, s4), waits=[vv])
            aa = OPN("scalar", lambda e: e.activation(r4, q4, AF.Sqrt), waits=[vv])
            vmr = OPN("vector", lambda e: e.tensor_scalar(mr4, m4, -1.0, None, OP.mult),
                      waits=[vm])
            tok = OPN("vector", lambda e: e.tensor_tensor(mr4, mr4, r4, OP.mult),
                      waits=[vmr, aa])
            outs = [(m4[:, j:j + 1], r4[:, j:j + 1], mr4[:, j:j + 1]) for j in range(4)]
            return tok, outs

        def ln_stats(src_ap, width, stT, i0, wait_list, eps=EPS_LN, consumed=()):
            m = col(stT, i0); r = col(stT, i0 + 1); mr = col(stT, i0 + 2)
            q1 = col(stT, i0 + 3); q2 = col(stT, i0 + 4)
            wl = list(wait_list) + [t for t in consumed if t is not None]
            vs = OPN("vector", lambda e: e.tensor_reduce(m, src_ap, AX.X, OP.add),
                     waits=wl)
            aa = OPN("scalar", lambda e: e.activation(actscr[:, 0:width],
                                                      src_ap, AF.Square, accum_out=q1),
                     waits=wl)
            vm = OPN("vector", lambda e: e.tensor_scalar(m, m, 1.0 / width, None,
                                                         OP.mult), waits=[vs])
            vv = OPN("vector", lambda e: e.tensor_scalar(q2, q1, 1.0 / width, None,
                                                         OP.mult), waits=[aa])
            vq = OPN("vector", lambda e: e.tensor_tensor(q1, m, m, OP.mult), waits=[vm])
            vv = OPN("vector", lambda e: e.tensor_tensor(q2, q2, q1, OP.subtract),
                     waits=[vq, vv])
            vv = OPN("vector", lambda e: e.tensor_scalar(q2, q2, eps, None, OP.add),
                     waits=[vv])
            vv = rsqrt_col(r, q2, q1, q2, srctok=vv)
            vmr = OPN("vector", lambda e: e.tensor_scalar(mr, m, -1.0, None, OP.mult),
                      waits=[vm])
            tok = OPN("vector", lambda e: e.tensor_tensor(mr, mr, r, OP.mult),
                      waits=[vmr, vv])
            return tok, m, r, mr

        def barrier():
            toks = [snap(c) for c in ["sd", "gd", "ad", "pe", "ve", "ae", "gc"]]
            toks = [t for t in toks if t is not None]
            for e in ENGINES:
                OPN(e, lambda eng: None, waits=toks)

        x2st = [None] * 64
        with ExitStack() as esp:
            SbP = lambda *a: esp.enter_context(nc.sbuf_tensor(*a))
            wld = SbP("wld", [128, 2, HID], F32)
            wqs = SbP("wqs", [128, HID], F32)
            wqb = SbP("wqb", [128, HID], BF16)
            wtp = SbP("wtp", [128, HID], BF16)
            acc32 = SbP("acc32", [128, 32], F32)
            accs = SbP("accs", [128, 1], F32)
            p_misc = esp.enter_context(nc.psum_tensor("p_misc", [128, 512], F32))

            def wquant(tiles, ntile, width, sw_slot, mw_slot, wqT_view, prev_busy=None,
                       prev_store=None):
                abs_toks = [None] * ntile
                ld_toks = [None] * ntile
                for t in range(ntile):
                    buf = wld[:, t % 2, 0:width]
                    w = [abs_toks[t - 2]] if t >= 2 else [prev_busy]
                    ld_toks[t] = OPN("gpsimd", lambda e, b=buf, t=t: e.dma_start(
                        out=b, in_=tiles[t]), waits=w, dma="gd")
                    abs_toks[t] = OPN("scalar", lambda e, b=buf, t=t, width=width:
                                      e.activation(wtp[:, 0:width], b, AF.Abs,
                                                   accum_out=col(acc32, t)),
                                      waits=[ld_toks[t]] +
                                            ([prev_store] if t == 0 else []))
                vr = OPN("vector", lambda e: e.tensor_reduce(
                    accs[:], acc32[:, 0:ntile], AX.X, OP.add), waits=[abs_toks[-1]])
                ms = OPN("tensor", lambda e: e.matmul(p_misc[0:1, 0:1], ones1[:],
                                                      accs[:], start=True, stop=True),
                         waits=[vr, c_o1])
                nelem = float(ntile * 128 * width)
                vm = OPN("vector", lambda e: e.tensor_scalar(
                    accs[0:1, :], p_misc[0:1, 0:1], 1.0 / nelem, EPS_Q, OP.mult, OP.max),
                    waits=[ms])
                mr2 = OPN("tensor", lambda e: e.matmul(p_misc[:, 1:2], onesr[:],
                                                       accs[0:1, :], start=True, stop=True),
                          waits=[vm, c_or])
                vmw = OPN("vector", lambda e: e.tensor_copy(
                    col(smallA, mw_slot), p_misc[:, 1:2]), waits=[mr2])
                vsw = OPN("vector", lambda e: e.reciprocal(
                    col(smallA, sw_slot), col(smallA, mw_slot)), waits=[vmw])
                st_tok = None
                tp_tok = None
                v3_prev = None
                for t in range(ntile):
                    buf = wld[:, t % 2, 0:width]
                    w = [v3_prev] if t >= 2 else [abs_toks[-2 + t]]
                    ld = OPN("gpsimd", lambda e, b=buf, t=t: e.dma_start(out=b, in_=tiles[t]),
                             waits=w, dma="gd")
                    v1 = OPN("vector", lambda e, b=buf: e.tensor_scalar(
                        wqs[:, 0:width], b, col(smallA, sw_slot), 1.0, OP.mult, OP.min),
                        waits=[ld, vsw] + ([tp_tok] if tp_tok else []))
                    v2 = OPN("vector", lambda e, width=width: e.tensor_scalar(
                        wqs[:, 0:width], wqs[:, 0:width], -1.0, RC, OP.max, OP.add),
                        waits=[v1])
                    v3 = OPN("vector", lambda e, width=width: e.tensor_scalar(
                        wqb[:, 0:width], wqs[:, 0:width], RC, None, OP.subtract),
                        waits=[v2])
                    v3_prev = v1
                    wquant._last_v1 = v1
                    tp_tok = dmaT(
                        wtp[:, 0:width].rearrange("p (c q) -> p c q", q=128),
                        wqb[:, 0:width], waits=[v3, st_tok])
                    st_tok = OPN("scalar", lambda e, t=t, width=width: e.dma_start(
                        out=wqT_view[:, :, t * 128:(t + 1) * 128],
                        in_=wtp[:, 0:width].rearrange("p (c q) -> p c q", q=128)),
                        waits=[tp_tok], dma="ad")
                return st_tok

            w1_done = wquant([w1_t[t] for t in range(32)], 32, D, SW1, MW1, w1qT_v)
            w2_done = wquant([w2_t[t] for t in range(8)], 8, HID, SW2, MW2, w2qT_v,
                             prev_busy=wquant._last_v1, prev_store=w1_done)
            barrier()

        def _phase_a():
          nonlocal x2st
          with ExitStack() as esa:
            SbA = lambda *a: esa.enter_context(nc.sbuf_tensor(*a))
            PsA = lambda *a: esa.enter_context(nc.psum_tensor(*a))
            WqS = SbA("WqS", [128, 8, D], BF16)
            WkS = SbA("WkS", [128, 8, D], BF16)
            WvS = SbA("WvS", [128, 8, D], BF16)
            WoS = SbA("WoS", [128, 8, D], BF16)
            proj2 = SbA("proj2", [128, F], BF16)
            bo_t = SbA("s_bo_t", [128, D], F32)
            ln1_fm = SbA("ln1_fm", [128, 8, 512], BF16)
            ln1_tm = SbA("ln1_tm", [128, 2, D], BF16)
            kqf = SbA("kqf", [128, 2, 512], BF16)
            kp_tm = SbA("kp_tm", [128, 4, 8 * F], BF16)
            v_aug = SbA("v_aug", [128, 4, H * 65], BF16)
            kv_acc = SbA("kv_acc", [65, H * F], F32)
            kv_bfT = SbA("kv_bfT", [128, 2 * H, 65], BF16)
            xth = SbA("xth", [128, 4, D], F32)
            xres = SbA("xres", [128, 2, D], F32)
            qpf = SbA("qpf", [128, 2, 2, 512], BF16)
            attn_fm = SbA("attn_fm", [128, 8, 512], BF16)
            nflat = SbA("nflat", [1, 16 * 512], BF16)
            n16 = SbA("n16", [8, 2, 512], BF16)
            n16f = SbA("n16f", [8, 2, 512], F32)
            g16 = SbA("g16", [8, 2, 512], BF16)
            x2o = SbA("x2o", [128, 2, D], F32)

            p_kv = PsA("p_kv", [128, 4 * 512], F32)
            p_a = PsA("p_a", [128, 1024], F32)
            p_b = PsA("p_b", [128, 1024], F32)

            def p_att(h):
                return p_kv[0:65, 512 * (h % 2):512 * (h % 2) + 512]

            ld_wq = OPN("gpsimd", lambda e: e.dma_start(out=WqS[:], in_=wq_v), dma="gd")
            ld_wk = OPN("gpsimd", lambda e: e.dma_start(out=WkS[:], in_=wk_v), dma="gd")
            ld_wv = OPN("gpsimd", lambda e: e.dma_start(out=WvS[:], in_=wv_v), dma="gd")
            ld_wo = OPN("gpsimd", lambda e: e.dma_start(out=WoS[:], in_=wo_v), dma="gd")
            ld_p1 = OPN("gpsimd", lambda e: e.dma_start(out=proj2[0:64, :], in_=proj_h), dma="gd")
            ld_p2 = OPN("gpsimd", lambda e: e.dma_start(out=proj2[64:128, :], in_=proj_h), dma="gd")
            ld_bo = OPN("sync", lambda e: e.dma_start(out=bo_t[:], in_=bo_h), dma="sd")
            ones_toks = []
            for sub in range(4):
                va = v_aug[:, sub].rearrange("p (h q) -> p h q", q=65)
                ones_toks.append(OPN("vector", lambda e, va=va: e.memset(va[:, :, 64:65], 1.0)))

            def emit_ln1(gt, prev_consumed, pa_free, lnfm_free=None):
                ldx = OPN("gpsimd", lambda e: e.dma_start(
                    out=xth[:], in_=x_t[4 * gt:4 * gt + 4].rearrange("j p d -> p j d")),
                    waits=list(prev_consumed), dma="gd")
                prev_ap = getattr(emit_ln1, "_prev_ap", None)
                tok, mrm = ln_stats4([xth[:, j] for j in range(4)], D, stB, 0,
                                     [[ldx]] * 4, consumed=[prev_ap])
                last_ap = None
                tpl_slot = getattr(emit_ln1, "_tpl", [None, None])
                ev = None
                for sub in range(4):
                    src = xth[:, sub]
                    m, r, mr = mrm[sub]
                    a_ap = OPN("scalar", lambda e, src=src, sub=sub, r=r, mr=mr:
                               e.activation(ln1_tm[:, sub % 2, :], src, AF.Identity,
                                            bias=mr, scale=r),
                               waits=[tok, tpl_slot[sub % 2]])
                    last_ap = a_ap
                    ev = dmaT(ln1_fm[:, :, 128 * sub:128 * sub + 128],
                              ln1_tm[:, sub % 2, :],
                              waits=[a_ap] + ([lnfm_free] if (sub == 0 and lnfm_free) else []))
                    tpl_slot[sub % 2] = ev
                emit_ln1._prev_ap = last_ap
                emit_ln1._tpl = tpl_slot
                return ev, ldx, (last_ap, tok)

            prev_ln_consumed = ()
            kvflush = None
            ev_kq_prev = None
            kv_mm_last = None
            pa_free = None
            lnfm_free = None
            vx2_slot = {}
            stx_slot = {}
            va_last = [None]

            ev_ln_cur, _ldx0, prev_ln_consumed = emit_ln1(0, prev_ln_consumed, None,
                                                          None)
            for sq in range(SEQ_PER_CORE):
                for t in range(NT):
                    gt = sq * NT + t
                    ev_ln = ev_ln_cur
                    for wave in range(2 if A_STAGE >= 2 else 0):
                        evk_tok = None
                        for cc in range(4):
                            c = 4 * wave + cc
                            mk = None
                            for ic in range(8):
                                w = []
                                if ic == 0:
                                    w = [ev_ln, snap("gd")] if (t == 0 and wave == 0 and cc == 0 and sq == 0) else [ev_ln]
                                    if ev_kq_prev is not None:
                                        w.append(ev_kq_prev)
                                    if (t == 0 and wave == 0 and cc == 0
                                            and va_last[0] is not None):
                                        w.append(va_last[0])
                                mk = OPN("tensor", lambda e, c=c, ic=ic, t=t: e.matmul(
                                    p_b[:, 512 * (c % 2):512 * (c % 2) + 512],
                                    WkS[:, ic, 128 * c:128 * (c + 1)],
                                    ln1_fm[:, ic, :],
                                    start=(ic == 0), stop=(ic == 7)),
                                    waits=(w if ic == 0 else []))
                            ev_kq = OPN("vector", lambda e, c=c: e.tensor_copy(
                                kqf[:, c % 2, :], p_b[:, 512 * (c % 2):512 * (c % 2) + 512]),
                                waits=[mk])
                            ev_kq_prev = ev_kq
                            if c == 7:
                                lnfm_free = mk
                            for sp in range(2 if A_KP else 0):
                                mkp = None
                                for su in range(2):
                                    sub = 2 * sp + su
                                    for par in range(2):
                                        w = []
                                        if su == 0 and par == 0:
                                            w = [ev_kq]
                                            if evk_tok is not None:
                                                w.append(evk_tok)
                                            if kv_mm_last is not None and cc == 0 and sp == 0:
                                                w.append(kv_mm_last)
                                        mkp = OPN("tensor", lambda e, c=c, sub=sub, par=par, su=su:
                                                  e.matmul(
                                            p_a[:, 512 * par + 256 * su:512 * par + 256 * su + 256],
                                            kqf[64 * par:64 * par + 64, c % 2,
                                                128 * sub:128 * sub + 128],
                                            proj2[64 * par:64 * par + 64, :],
                                            start=True, stop=True),
                                            waits=w)
                                hh = (2 * c) % 8
                                evk_tok = OPN("vector", lambda e, sp=sp, hh=hh: e.tensor_relu(
                                    kp_tm[:, 2 * sp:2 * sp + 2, 256 * hh:256 * hh + 512]
                                        .rearrange("p su (hd q) -> p su hd q", q=256),
                                    p_a[:].rearrange("p (par su q) -> p su par q",
                                                     par=2, su=2)),
                                    waits=[mkp])
                        if wave == 0 and A_STAGE >= 3:
                            evv = None
                            for sub in range(4):
                                mv = None
                                for ic in range(8):
                                    for hf in range(2):
                                        w = []
                                        if ic == 0 and hf == 0:
                                            w = [evk_tok if sub == 0 else evv, ev_ln]
                                        mv = OPN("tensor", lambda e, sub=sub, ic=ic, hf=hf, t=t:
                                                 e.matmul(
                                            p_a[:, 512 * hf:512 * hf + 512],
                                            ln1_fm[:, ic, 128 * sub:128 * sub + 128],
                                            WvS[:, ic, 512 * hf:512 * hf + 512],
                                            start=(ic == 0), stop=(ic == 7)),
                                            waits=(w if (ic == 0 and hf == 0) else []))
                                evv = OPN("vector", lambda e, sub=sub: e.tensor_copy(
                                    v_aug[:, sub].rearrange("p (h q) -> p h q", q=65)[:, :, 0:64],
                                    p_a[:].rearrange("p (h q) -> p h q", q=64)),
                                    waits=[mv, ones_toks[sub]])
                            evk_tok = evv
                        for hh in range(8 if A_STAGE >= 4 else 0):
                            h = 8 * wave + hh
                            off = 512 * (hh // 2) + 256 * (hh % 2)
                            for sub in range(4):
                                w = []
                                if hh == 0 and sub == 0:
                                    w = [evv if wave == 0 else evk_tok]
                                    if kvflush is not None:
                                        w.append(kvflush)
                                kv_mm_last = OPN("tensor", lambda e, h=h, hh=hh, off=off, sub=sub:
                                                 e.matmul(
                                    p_kv[0:65, off:off + 256],
                                    v_aug[:, sub, 65 * h:65 * h + 65],
                                    kp_tm[:, sub, 256 * hh:256 * hh + 256],
                                    start=(sub == 0), stop=(sub == 3)),
                                    waits=w)
                        for bank in range(4 if A_STAGE >= 4 else 0):
                            pair0 = 8 * wave + 2 * bank
                            dst = kv_acc[:, F * pair0:F * pair0 + 512]
                            src = p_kv[0:65, 512 * bank:512 * bank + 512]
                            w = [kv_mm_last] if bank == 0 else []
                            if t == 0:
                                kvflush = OPN("vector", lambda e, dst=dst, src=src:
                                              e.tensor_copy(dst, src), waits=w)
                            else:
                                kvflush = OPN("vector", lambda e, dst=dst, src=src:
                                              e.tensor_tensor(dst, dst, src, OP.add),
                                              waits=w)
                    if A_STAGE >= 3:
                        pa_free = evv
                    elif A_STAGE >= 2:
                        pa_free = evk_tok if A_KP else None
                    nxt = (gt + 1) if t < NT - 1 else (sq * NT if A_PASS2 else None)
                    if nxt is not None:
                        ev_ln_cur, _, prev_ln_consumed = emit_ln1(
                            nxt, prev_ln_consumed, pa_free, lnfm_free)
                    if DBG_OUTS and sq == 0 and t == NT - 1:
                        wd = [snap("ve"), snap("pe")]
                        OPN("sync", lambda e: e.dma_start(out=lndbg_h, in_=ln1_fm[:]), waits=wd, dma="sd")
                        OPN("sync", lambda e: e.dma_start(out=kpdbg_h, in_=kp_tm[:]), waits=wd, dma="sd")
                        OPN("sync", lambda e: e.dma_start(out=vadbg_h, in_=v_aug[:]), waits=wd, dma="sd")
                        OPN("sync", lambda e: e.dma_start(out=kqdbg_h, in_=kqf[:]), waits=wd, dma="sd")
                        OPN("sync", lambda e: e.dma_start(out=xthdbg_h, in_=xth[:]), waits=wd + [snap("gd")], dma="sd")
                        OPN("sync", lambda e: e.dma_start(out=stadbg_h, in_=stA[:]), waits=wd + [snap("ae")], dma="sd")

                if not A_KVT:
                    continue
                evt = None
                for grp in range(8):
                    trt = None
                    for j in range(4):
                        i = 4 * grp + j
                        w = ([kvflush, pa_free] if (grp == 0 and j == 0)
                             else ([evt] if (j == 0 and evt) else []))
                        trt = OPN("tensor", lambda e, i=i, j=j: e.transpose(
                            p_a[:, 128 * j:128 * j + 65],
                            kv_acc[:, 128 * i:128 * (i + 1)], identf[0:65, 0:65]),
                            waits=[t for t in w if t])
                    evt = OPN("vector", lambda e, grp=grp: e.tensor_copy(
                        kv_bfT[:, 4 * grp:4 * grp + 4, :],
                        p_a[:, 0:512].rearrange("p (j q) -> p j q", q=128)[:, :, 0:65]),
                        waits=[trt])
                pa_free = evt
                if DBG_OUTS and sq == 0:
                    OPN("sync", lambda e: e.dma_start(out=kvdbg_h, in_=kv_acc[:]),
                        waits=[kvflush], dma="sd")

                for t in range(NT if A_PASS2 else 0):
                    gt = sq * NT + t
                    ev_ln = ev_ln_cur
                    ga_prev = getattr(emit_ln1, "_ga15", None)
                    ga_toks = [None] * 16
                    eva_toks = [None] * 16
                    evqp_prev = None
                    evqp_prev2 = {}
                    st_n_prev = getattr(emit_ln1, "_stn", None)
                    ev_q_prev = None
                    ma_tok = None
                    for c in range(8):
                        mq = None
                        for ic in range(8):
                            w = []
                            if ic == 0:
                                w = [ev_ln]
                                if c == 0 and pa_free is not None:
                                    w.append(pa_free)
                                if ev_q_prev is not None:
                                    w.append(ev_q_prev)
                            mq = OPN("tensor", lambda e, c=c, ic=ic: e.matmul(
                                p_a[:, 512 * (c % 2):512 * (c % 2) + 512],
                                WqS[:, ic, 128 * c:128 * (c + 1)],
                                ln1_fm[:, ic, :],
                                start=(ic == 0), stop=(ic == 7)),
                                waits=(w if ic == 0 else []))
                        ev_q = OPN("vector", lambda e, c=c: e.tensor_copy(
                            kqf[:, c % 2, :], p_a[:, 512 * (c % 2):512 * (c % 2) + 512]),
                            waits=[mq])
                        ev_q_prev = ev_q
                        if c == 7:
                            lnfm_free = mq
                        for par in range(2):
                            h = 2 * c + par
                            mqp = None
                            qp_dst = p_b[:, 0:1024] if h % 2 == 0 else p_kv[:, 1024:2048]
                            for half in range(2):
                                w = []
                                if half == 0:
                                    w = [ev_q]
                                    if evqp_prev2.get(h % 2) is not None:
                                        w.append(evqp_prev2[h % 2])
                                mqp = OPN("tensor", lambda e, qp_dst=qp_dst, c=c, par=par, half=half:
                                          e.matmul(
                                    qp_dst[:, 512 * half:512 * half + 512],
                                    proj2[64 * par:64 * par + 64, 128 * half:128 * half + 128],
                                    kqf[64 * par:64 * par + 64, c % 2, :],
                                    start=True, stop=True), waits=w)
                            evqp = OPN("vector", lambda e, h=h, qp_dst=qp_dst: e.tensor_relu(
                                qpf[:, h % 2].rearrange("p a b -> p (a b)"), qp_dst),
                                waits=[mqp])
                            evqp_prev2[h % 2] = evqp
                            evqp_prev = evqp
                            for half in range(2):
                                w = []
                                if half == 0:
                                    w = [evqp]
                                    if h >= 2:
                                        w.append(eva_toks[h - 2])
                                        w.append(ga_toks[h - 2])
                                    elif ga_prev is not None:
                                        w.append(ga_prev)
                                ma_tok = OPN("tensor", lambda e, h=h, half=half: e.matmul(
                                    p_att(h), kv_bfT[:, 2 * h + half, :],
                                    qpf[:, h % 2, half, :],
                                    start=(half == 0), stop=(half == 1)), waits=w)
                            gw = [ma_tok]
                            if h == 0 and st_n_prev is not None:
                                gw.append(st_n_prev)
                            ga_toks[h] = OPN("scalar", lambda e, h=h: e.activation(
                                nflat[0:1, 512 * h:512 * h + 512], p_att(h)[64:65, :],
                                AF.Identity, bias=0.0, scale=1.0), waits=gw)
                            eva_toks[h] = OPN("vector", lambda e, h=h: e.tensor_copy(
                                attn_fm[64 * (h % 2):64 * (h % 2) + 64, h // 2, :],
                                p_att(h)[0:64, :]), waits=[ma_tok])
                    nxt = (gt + 1) if t < NT - 1 else (
                        (sq + 1) * NT if sq + 1 < SEQ_PER_CORE else None)
                    if nxt is not None:
                        ev_ln_cur, _, prev_ln_consumed = emit_ln1(
                            nxt, prev_ln_consumed, pa_free, lnfm_free)
                    vg_half = [None, None]
                    prev_vg = getattr(emit_ln1, "_vg", None)
                    for hb in range(2):
                        st_n = OPN("sync", lambda e, gt=gt, hb=hb: e.dma_start(
                            out=norm_h[gt % 2, hb].rearrange("h t -> (h t)")[None, :],
                            in_=nflat[0:1, 4096 * hb:4096 * hb + 4096]),
                            waits=[ga_toks[8 * hb + 7]], dma="sd")
                        if hb == 1:
                            emit_ln1._stn = st_n
                            emit_ln1._ga15 = ga_toks[15]
                        ld_n = OPN("sync", lambda e, gt=gt, hb=hb: e.dma_start(
                            out=n16[:, hb, :], in_=norm_h[gt % 2, hb]),
                            waits=[st_n, prev_vg], dma="sd")
                        vg = OPN("vector", lambda e, hb=hb: e.tensor_scalar(
                            n16f[:, hb, :], n16[:, hb, :],
                            EPS_PF, None, OP.add), waits=[ld_n])
                        vg = OPN("vector", lambda e, hb=hb: e.reciprocal(
                            n16f[:, hb, :], n16f[:, hb, :]),
                            waits=[vg])
                        vg = OPN("vector", lambda e, hb=hb: e.tensor_copy(
                            g16[:, hb, :], n16f[:, hb, :]),
                            waits=[vg])
                        vg_half[hb] = vg
                    vg = vg_half[1]
                    emit_ln1._vg = vg
                    va_tok = None
                    va_list = []
                    for c in range(8):
                        w = ([vg_half[0], c_sel, evqp_prev2[0]] if c == 0 else
                             ([va_tok, vg_half[1]] if c == 4 else [va_tok])) + \
                            [eva_toks[2 * c], eva_toks[2 * c + 1]]
                        mg = OPN("tensor", lambda e, c=c: e.matmul(
                            p_b[:, 0:512], sel[:, 128 * c:128 * (c + 1)],
                            g16[:, c // 4, :],
                            start=True, stop=True), waits=w)
                        va_tok = OPN("vector", lambda e, c=c: e.tensor_tensor(
                            attn_fm[:, c, :], attn_fm[:, c, :], p_b[:, 0:512], OP.mult),
                            waits=[mg])
                        va_list.append(va_tok)
                        va_last[0] = va_tok
                    vz_prev = None
                    for sub in range(4):
                        ldxr = OPN("scalar", lambda e, gt=gt, sub=sub: e.dma_start(
                            out=xres[:, sub % 2], in_=x_t[4 * gt + sub]),
                            waits=[vx2_slot.get(sub % 2)], dma="ad")
                        mo = None
                        for ic in range(8):
                            for hf in range(2):
                                w = []
                                if hf == 0 and sub == 0:
                                    w = [va_list[ic]]
                                if ic == 0 and hf == 0 and vz_prev is not None:
                                    w.append(vz_prev)
                                mo = OPN("tensor", lambda e, sub=sub, ic=ic, hf=hf: e.matmul(
                                    p_a[:, 512 * hf:512 * hf + 512],
                                    attn_fm[:, ic, 128 * sub:128 * sub + 128],
                                    WoS[:, ic, 512 * hf:512 * hf + 512],
                                    start=(ic == 0), stop=(ic == 7)),
                                    waits=(w if w else []))

                        zb = actscr[:, 1024:2048]
                        vz = OPN("vector", lambda e: e.tensor_tensor(
                            zb, p_a[:], bo_t[:], OP.add), waits=[mo, ld_bo])
                        vz_prev = vz
                        tok2, m2, r2, mr2 = ln_stats(zb, D, stA, 6, [vz],
                                                     consumed=[getattr(emit_ln1, "_a2p", None)])
                        a2 = OPN("scalar", lambda e, r2=r2, mr2=mr2: e.activation(
                            zb, zb, AF.Identity, bias=mr2, scale=r2), waits=[tok2])
                        emit_ln1._a2p = a2
                        vx2 = OPN("vector", lambda e, sub=sub: e.tensor_tensor(
                            x2o[:, sub % 2], zb, xres[:, sub % 2], OP.add),
                            waits=[a2, ldxr, stx_slot.get(sub % 2)])
                        vx2_slot[sub % 2] = vx2
                        stx = OPN("sync", lambda e, gt=gt, sub=sub: e.dma_start(
                            out=x2_t[4 * gt + sub], in_=x2o[:, sub % 2]),
                            waits=[vx2], dma="sd")
                        stx_slot[sub % 2] = stx
                    x2st[gt] = stx
                    pa_free = vz_prev
                    if DBG_OUTS and gt == 0:
                        OPN("sync", lambda e: e.dma_start(out=afdbg_h, in_=attn_fm[:]),
                            waits=[va_tok], dma="sd")
                        OPN("sync", lambda e: e.dma_start(out=qpdbg_h, in_=qpf[:]),
                            waits=[va_tok], dma="sd")
            barrier()

        def _phase_b():
          with ExitStack() as esb:
            SbB = lambda *a: esb.enter_context(nc.sbuf_tensor(*a))
            PsB = lambda *a: esb.enter_context(nc.psum_tensor(*a))
            w1qT = SbB("w1qT", [128, 8, HID], BF16)
            w2qT = SbB("w2qT", [128, 32, D], BF16)
            n1w = SbB("n1w", [128, D], F32)
            n2w = SbB("n2w", [128, HID], F32)
            x2b = SbB("x2b", [128, 2, D], F32)
            shf = SbB("shf", [128, D], F32)
            y2f = SbB("y2f", [128, D], F32)
            xq_tm = SbB("xq_tm", [128, D], BF16)
            xq_fm = SbB("xq_fm", [128, 8, 128], BF16)
            g_sb = SbB("g_sb", [128, HID], BF16)
            u2b = SbB("u2b", [128, 2, 512], F32)
            hq_tm = SbB("hq_tm", [128, 2, 1024], BF16)
            hq_fm = SbB("hq_fm", [128, 32, 128], BF16)
            outb = SbB("outb", [128, 1, D], F32)

            p_h = PsB("p_h", [128, 4, 512], F32)
            p_y = PsB("p_y", [128, 1024], F32)
            p_x = PsB("p_x", [128, 1024], BF16)

            rrb2 = (actscr[:, 0:512], actscr[:, 512:1024])
            ld_w1q = OPN("sync", lambda e: e.dma_start(out=w1qT[:], in_=w1qT_v),
                         waits=[w1_done], dma="sd")
            ld_w2q = OPN("sync", lambda e: e.dma_start(out=w2qT[:], in_=w2qT_v),
                         waits=[w2_done], dma="sd")
            ld_n1 = OPN("sync", lambda e: e.dma_start(out=n1w[:], in_=n1w_h), dma="sd")
            ld_n2 = OPN("sync", lambda e: e.dma_start(out=n2w[:], in_=n2w_h), dma="sd")

            a_r_bank = [None] * 4
            a_y2_prev = [None]
            v_h3_last = [None]
            a_t_prev = {}
            tph_slot = [None, None]
            v_out_prev = {}
            my_prev = [None]
            st_y = None

            def part1(blk):
                bb = blk % 2
                ldb = OPN("gpsimd", lambda e, blk=blk, bb=bb: e.dma_start(
                    out=x2b[:, bb], in_=x2_t[blk]),
                    waits=[x2st[blk // 4], v_out_prev.get(bb), a_t_prev.get(bb)],
                    dma="gd")
                tok3, m3, r3, mr3 = ln_stats(x2b[:, bb], D, stB, 0, [ldb],
                                             consumed=[v_h3_last[0]])
                c1 = col(stB, 5); tq = col(stB, 6); tq2 = col(stB, 7)
                vv = OPN("vector", lambda e, r3=r3: e.tensor_tensor(tq, r3, r3, OP.mult),
                         waits=[tok3])
                vv = OPN("vector", lambda e: e.tensor_scalar(
                    tq, tq, -EPS_LN, 1.0 + EPS_NORM, OP.mult, OP.add), waits=[vv])
                v_c1 = rsqrt_col(c1, tq, tq2, col(stB, 8), srctok=vv)
                a_t = OPN("scalar", lambda e, r3=r3, mr3=mr3, bb=bb: e.activation(
                    shf[:], x2b[:, bb], AF.Identity, bias=mr3, scale=r3),
                    waits=[tok3, a_y2_prev[0]])
                a_t_prev[bb] = a_t
                v_u = OPN("vector", lambda e: e.tensor_tensor(shf[:], shf[:], n1w[:],
                                                              OP.mult),
                          waits=[a_t, ld_n1])
                am = col(stB, 9); A1 = col(stB, 10); cs = col(stB, 11); f1 = col(stB, 12)
                v_am = OPN("vector", lambda e: e.tensor_reduce(
                    am, shf[:], AX.X, OP.max, apply_absolute_value=True), waits=[v_u])
                vv = OPN("vector", lambda e: e.tensor_tensor(A1, c1, am, OP.mult),
                         waits=[v_c1, v_am])
                vv = OPN("vector", lambda e: e.tensor_scalar(A1, A1, EPS_Q, None, OP.max),
                         waits=[vv])
                vv = OPN("vector", lambda e: e.reciprocal(cs, A1), waits=[vv])
                vv = OPN("vector", lambda e: e.tensor_scalar(cs, cs, 127.0, None, OP.mult),
                         waits=[vv])
                v_cs = OPN("vector", lambda e: e.tensor_tensor(cs, cs, c1, OP.mult),
                           waits=[vv])
                vv = OPN("vector", lambda e: e.tensor_tensor(
                    f1, A1, col(smallA, MW1), OP.mult), waits=[v_cs])
                v_f1 = OPN("vector", lambda e: e.tensor_scalar(
                    f1, f1, 1.0 / 127.0, None, OP.mult), waits=[vv])
                a_xq = OPN("scalar", lambda e, cs=cs: e.activation(
                    actscr[:, 1024:2048], shf[:], AF.Identity, bias=rc_col, scale=cs),
                    waits=[v_cs, v_rc])
                v_xq = OPN("vector", lambda e: e.tensor_scalar(
                    xq_tm[:], actscr[:, 1024:2048], RC, None, OP.subtract), waits=[a_xq])
                trx = None
                for c in range(8):
                    w = [v_xq, c_idb] if c == 0 else []
                    trx = OPN("tensor", lambda e, c=c: e.transpose(
                        p_x[:, 128 * c:128 * (c + 1)], xq_tm[:, 128 * c:128 * (c + 1)],
                        identb[:]), waits=(w if c == 0 else []))
                v_xf = OPN("vector", lambda e: e.tensor_copy(
                    xq_fm[:].rearrange("p c q -> p (c q)"), p_x[:]), waits=[trx])
                return dict(bb=bb, f1=f1, v_f1=v_f1, v_xf=v_xf)

            def part23(blk, P):
                f1 = P["f1"]; v_f1 = P["v_f1"]; v_xf = P["v_xf"]
                sl_red = [None] * 8
                v_gs_sl = [None] * 8
                a_q_tok = None
                a_st_tok = None
                v_gs = None
                for s in range(8):
                    mh = None
                    for ic in range(8):
                        w = []
                        if ic == 0:
                            w = [v_xf, ld_w1q]
                            if a_r_bank[s % 4] is not None:
                                w.append(a_r_bank[s % 4])
                        mh = OPN("tensor", lambda e, s=s, ic=ic: e.matmul(
                            p_h[:, s % 4, :], xq_fm[:, ic, :],
                            w1qT[:, ic, 512 * s:512 * s + 512],
                            start=(ic == 0), stop=(ic == 7)),
                            waits=(w if ic == 0 else []))
                    sp = s % 2
                    a_r = OPN("scalar", lambda e, s=s, sp=sp, f1=f1: e.activation(
                        rrb2[sp], p_h[:, s % 4, :], AF.Relu, scale=f1),
                        waits=[mh, v_f1] + ([] if s >= 2 else
                                            [v_out_prev.get((blk - 1) % 2)]))
                    a_r_bank[s % 4] = a_r
                    a_q_tok = OPN("scalar", lambda e, sp=sp: e.activation(
                        u2b[:, sp, :], rrb2[sp], AF.Square, scale=sqa),
                        waits=[v_sqa] + ([v_gs_sl[s - 2]] if s >= 2 else []))
                    v_gs = OPN("vector", lambda e, s=s, sp=sp: e.tensor_scalar(
                        g_sb[:, 512 * s:512 * s + 512], u2b[:, sp, :], sga, col(ab, 1),
                        OP.mult, OP.add), waits=[a_q_tok, v_sga])
                    v_gs_sl[s] = v_gs
                    if s >= 2:
                        a_st_tok = OPN("scalar", lambda e, s2=s - 2: e.activation(
                            actscr[:, 0:512], g_sb[:, 512 * s2:512 * s2 + 512],
                            AF.Square, accum_out=stB[:, 13 + s2:14 + s2]),
                            waits=[v_gs_sl[s - 2]])
                    v_u2 = OPN("vector", lambda e, s=s, sp=sp: e.tensor_tensor(
                        actscr[:, 1024 + 512 * sp:1536 + 512 * sp],
                        g_sb[:, 512 * s:512 * s + 512],
                        n2w[:, 512 * s:512 * s + 512], OP.mult),
                        waits=[ld_n2, v_gs])
                    sl_red[s] = OPN("vector", lambda e, s=s, sp=sp: e.tensor_reduce(
                        stA[:, s:s + 1], actscr[:, 1024 + 512 * sp:1536 + 512 * sp],
                        AX.X, OP.max,
                        apply_absolute_value=True), waits=[v_u2])
                for s2 in (6, 7):
                    a_st_tok = OPN("scalar", lambda e, s2=s2: e.activation(
                        actscr[:, 0:512], g_sb[:, 512 * s2:512 * s2 + 512],
                        AF.Square, accum_out=stB[:, 13 + s2:14 + s2]),
                        waits=[v_gs_sl[s2]])
                c2 = col(stB, 5); A2 = col(stB, 6); cs2 = col(stB, 7); f2 = col(stB, 18)
                tqa = col(stB, 21); tqb = col(stB, 22)
                vv = OPN("vector", lambda e: e.tensor_reduce(
                    tqa, stB[:, 13:21], AX.X, OP.add), waits=[a_st_tok, sl_red[7]])
                vv = OPN("vector", lambda e: e.tensor_scalar(
                    tqa, tqa, 1.0 / HID, EPS_NORM, OP.mult, OP.add), waits=[vv])
                v_c2 = rsqrt_col(c2, tqa, tqb, col(stB, 23), srctok=vv)
                vA = OPN("vector", lambda e: e.tensor_reduce(
                    A2, stA[:, 0:8], AX.X, OP.max), waits=[sl_red[7]])
                vv = OPN("vector", lambda e: e.tensor_tensor(A2, A2, c2, OP.mult),
                         waits=[v_c2, vA])
                vv = OPN("vector", lambda e: e.tensor_scalar(A2, A2, EPS_Q, None, OP.max),
                         waits=[vv])
                vv = OPN("vector", lambda e: e.reciprocal(cs2, A2), waits=[vv])
                vv = OPN("vector", lambda e: e.tensor_scalar(cs2, cs2, 127.0, None,
                                                             OP.mult), waits=[vv])
                v_cs2 = OPN("vector", lambda e: e.tensor_tensor(cs2, cs2, c2, OP.mult),
                            waits=[vv])
                vv = OPN("vector", lambda e: e.tensor_tensor(
                    f2, A2, col(smallA, MW2), OP.mult), waits=[v_cs2])
                v_f2 = OPN("vector", lambda e: e.tensor_scalar(
                    f2, f2, 1.0 / 127.0, None, OP.mult), waits=[vv])
                tph_pair = [None] * 4
                for s in range(8):
                    sp = s % 2
                    w = [v_cs2] if s == 0 else []
                    v_h1 = OPN("vector", lambda e, s=s, sp=sp, cs2=cs2:
                               e.scalar_tensor_tensor(
                        u2b[:, sp, :], g_sb[:, 512 * s:512 * s + 512], cs2,
                        n2w[:, 512 * s:512 * s + 512], OP.mult, OP.mult), waits=w)
                    p2 = s // 2
                    v_h3 = OPN("vector", lambda e, s=s, sp=sp, p2=p2: e.tensor_scalar(
                        hq_tm[:, p2 % 2, 512 * (s % 2):512 * (s % 2) + 512],
                        u2b[:, sp, :], RC, RC, OP.add, OP.subtract),
                        waits=[v_h1, tph_slot[p2 % 2]])
                    v_h3_last[0] = v_h3
                    if s % 2 == 1:
                        wt = [v_h3]
                        if p2 <= 1 and my_prev[0] is not None:
                            wt.append(my_prev[0])
                        tph = dmaT(hq_fm[:, 8 * p2:8 * p2 + 8, :],
                                   hq_tm[:, p2 % 2, :], waits=wt)
                        tph_slot[p2 % 2] = tph
                        tph_pair[p2] = tph
                return dict(tph_pair=tph_pair, v_f2=v_f2, f2=f2)

            def part4a(blk, P2):
                my = None
                for ic in range(32):
                    for hf in range(2):
                        w = []
                        if hf == 0 and ic % 8 == 0:
                            w = [P2["tph_pair"][ic // 8]]
                            if ic == 0:
                                w.append(ld_w2q)
                                if a_y2_prev[0] is not None:
                                    w.append(a_y2_prev[0])
                        my = OPN("tensor", lambda e, ic=ic, hf=hf: e.matmul(
                            p_y[:, 512 * hf:512 * hf + 512], hq_fm[:, ic, :],
                            w2qT[:, ic, 512 * hf:512 * hf + 512],
                            start=(ic == 0), stop=(ic == 31)),
                            waits=(w if w else []))
                my_prev[0] = my
                return my

            def part4b(blk, P, P2, my):
                nonlocal st_y
                bb = P["bb"]
                a_y2 = OPN("scalar", lambda e, f2=P2["f2"]: e.activation(
                    y2f[:], p_y[:], AF.Identity, scale=f2), waits=[my, P2["v_f2"]])
                a_y2_prev[0] = a_y2
                tok4, m4, r4, mr4 = ln_stats(y2f[:], D, stB, 13, [a_y2])
                a_y4 = OPN("scalar", lambda e, r4=r4, mr4=mr4: e.activation(
                    actscr[:, 0:1024], y2f[:], AF.Identity, bias=mr4, scale=r4),
                    waits=[tok4])
                v_out = OPN("vector", lambda e, bb=bb: e.tensor_tensor(
                    outb[:, 0], actscr[:, 0:1024], x2b[:, bb], OP.add),
                    waits=[a_y4, st_y])
                v_out_prev[bb] = v_out
                st_y = OPN("scalar", lambda e, blk=blk: e.dma_start(
                    out=y_t[blk], in_=outb[:, 0]), waits=[v_out], dma="ad")

            P = part1(0)
            for blk in range(NBLK):
                P2 = part23(blk, P)
                my = part4a(blk, P2)
                Pn = part1(blk + 1) if blk + 1 < NBLK else None
                part4b(blk, P, P2, my)
                P = Pn

        if PHASE_LIMIT >= 2:
            _phase_a()
        if PHASE_LIMIT >= 3:
            _phase_b()
        OPN("sync", lambda e: None,
            waits=[snap("sd"), snap("ad"), snap("gd")])
        OPN("scalar", lambda e: None, waits=[snap("ad")])

        @block.sync
        def _(sync):
            bld.emit("sync", sync)

        @block.tensor
        def _(tensor):
            bld.emit("tensor", tensor)

        @block.vector
        def _(vector):
            bld.emit("vector", vector)

        @block.scalar
        def _(scalar):
            bld.emit("scalar", scalar)

        @block.gpsimd
        def _(gpsimd):
            bld.emit("gpsimd", gpsimd)

    return nc


def kernel(**inputs):
    x = np.ascontiguousarray(np.asarray(inputs["L0_attn_hidden_states"],
                                        dtype=np.float32).reshape(B * S, D))
    shards = np.split(x, NCORE, axis=0)
    bo_t = np.tile(np.asarray(inputs["L0_attn_bo"], np.float32).reshape(1, D), (128, 1))
    n1w_t = np.tile(np.asarray(inputs["L0_ffn_norm1_w"], np.float32).reshape(1, D), (128, 1))
    n2w_t = np.tile(np.asarray(inputs["L0_ffn_norm2_w"], np.float32).reshape(1, HID), (128, 1))
    a_s = np.asarray(inputs["L0_ffn_act_scale"], np.float32).reshape(1)
    b_s = np.asarray(inputs["L0_ffn_act_bias"], np.float32).reshape(1)
    ab_t = np.tile(np.concatenate([a_s, b_s]).reshape(1, 2), (128, 1))
    identb = np.eye(128, dtype=ml_dtypes.bfloat16)
    identf = np.eye(128, dtype=np.float32)
    sel = np.zeros((8, 8 * 128), dtype=ml_dtypes.bfloat16)
    for c in range(8):
        for p in range(128):
            sel[(2 * c + (p >= 64)) % 8, 128 * c + p] = 1.0
    common = {
        "wq": np.ascontiguousarray(inputs["L0_attn_Wq"], dtype=np.float32),
        "wk": np.ascontiguousarray(inputs["L0_attn_Wk"], dtype=np.float32),
        "wv": np.ascontiguousarray(inputs["L0_attn_Wv"], dtype=np.float32),
        "wo": np.ascontiguousarray(inputs["L0_attn_Wo"], dtype=np.float32),
        "proj": np.ascontiguousarray(inputs["L0_attn_proj"], dtype=np.float32),
        "bo_t": np.ascontiguousarray(bo_t), "n1w_t": np.ascontiguousarray(n1w_t),
        "n2w_t": np.ascontiguousarray(n2w_t), "ab_t": np.ascontiguousarray(ab_t),
        "w1": np.ascontiguousarray(inputs["L0_ffn_w1"], dtype=np.float32),
        "w2": np.ascontiguousarray(inputs["L0_ffn_w2"], dtype=np.float32),
        "identb": identb, "identf": identf, "sel": sel,
        "ones1": np.ones((128, 1), np.float32), "onesr": np.ones((1, 128), np.float32),
    }
    nc = build_nc()
    in_maps = [dict(common, x=np.ascontiguousarray(s)) for s in shards]
    res = run_bass_kernel_spmd(nc, in_maps, core_ids=list(range(NCORE)))
    out = np.concatenate([np.asarray(r["y"]) for r in res.results], axis=0)
    return out.reshape(B, S, D)
